# version 09
# baseline (speedup 1.0000x reference)
import numpy as np
import ml_dtypes
from contextlib import ExitStack
import concourse.bass as bass
import concourse.mybir as mybir
from concourse.bass_utils import run_bass_kernel_spmd

F32 = mybir.dt.float32
BF16 = mybir.dt.bfloat16
AF = mybir.ActivationFunctionType
ALU = mybir.AluOpType
AX = mybir.AxisListType

D = 4096
NDC = 32
T_OWN = 1024
T_HALO = 512
T_ALL = 1536
NT_ALL = 12
NT_OWN = 8
NQH = 32
NKV = 8
HD = 128
WIN = 5
NU = NQH * NT_OWN
QSCALE = float(1.0 / np.sqrt(128.0))


class DmaCtr:

    def __init__(self, nc, name):
        self.sems = [nc.alloc_semaphore(name + "_0"), nc.alloc_semaphore(name + "_1")]
        self.vals = [0, 0]

    def inc(self, inst, i):
        inst.then_inc(self.sems[i % 2], 16)
        self.vals[i % 2] += 16

    def wait(self, eng, i):
        if i < 0:
            return
        eng.wait_ge(self.sems[i % 2], 16 * (i // 2 + 1))

    def wait_all(self, eng, n):
        eng.wait_ge(self.sems[0], 16 * ((n + 1) // 2))
        if n > 1:
            eng.wait_ge(self.sems[1], 16 * (n // 2))


class Ctr:
    def __init__(self, nc, name):
        self.sem = nc.alloc_semaphore(name)
        self.val = 0

    def inc(self, inst, n=1):
        inst.then_inc(self.sem, n)
        self.val += n
        return self.val


def build_nc(phases=6, dump=False, sub=99, dumpscratch=False):
    nc = bass.Bass("TRN2", target_bir_lowering=False,
                   dynamic_dma_scratch_size=16384)

    xs = nc.dram_tensor("xs", [T_ALL, D], F32, kind="ExternalInput").ap()
    wqt = nc.dram_tensor("wqt", [NQH, 128, NDC, 128], F32, kind="ExternalInput").ap()
    wkt = nc.dram_tensor("wkt", [NKV, 128, NDC, 128], F32, kind="ExternalInput").ap()
    wvt = nc.dram_tensor("wvt", [NKV, 128, NDC, 128], F32, kind="ExternalInput").ap()
    wot = nc.dram_tensor("wot", [8, 32, 128, 512], F32, kind="ExternalInput").ap()
    wbt = nc.dram_tensor("wbt", [8, NDC, 128, 512], F32, kind="ExternalInput").ap()
    hct = nc.dram_tensor("hct", [NDC, 128, 137], F32, kind="ExternalInput").ap()
    bTd = nc.dram_tensor("bTd", [128, 4096], F32, kind="ExternalInput").ap()
    bbd = nc.dram_tensor("bbd", [128, 4096], F32, kind="ExternalInput").ap()
    btrd = nc.dram_tensor("btrd", [128, 1], F32, kind="ExternalInput").ap()
    mskd = nc.dram_tensor("mskd", [128, NT_OWN, WIN, 128], BF16, kind="ExternalInput").ap()
    idend = nc.dram_tensor("idend", [128, 128], BF16, kind="ExternalInput").ap()
    y = nc.dram_tensor("y", [T_OWN, D], F32, kind="ExternalOutput").ap()
    _sk = "ExternalOutput" if dumpscratch else "Internal"
    o_s = nc.dram_tensor("o_s", [NQH, NT_OWN, 128, HD], BF16, kind=_sk).ap()
    attn_s = nc.dram_tensor("attn_s", [T_OWN, D], F32, kind=_sk).ap()
    x1_s = nc.dram_tensor("x1_s", [T_OWN, D], F32, kind=_sk).ap()
    ffn_s = nc.dram_tensor("ffn_s", [T_OWN, D], F32, kind=_sk).ap()

    C = lambda n: Ctr(nc, n)
    c_ldx = DmaCtr(nc, "ldx"); c_vbs = C("vbs"); c_aln = C("aln"); c_vini = C("vini"); c_vbn = C("vbn"); c_ainv = C("ainv"); c_vnmi = C("vnmi")
    c_axn = C("axn"); c_txt = C("txt"); c_vxe = C("vxe")
    c_gwkv = DmaCtr(nc, "gwkv"); c_tkv = C("tkv"); c_akv = C("akv")
    c_tvt = C("tvt"); c_vve = C("vve")
    c_ldm = C("ldm"); c_ldi = C("ldi"); c_ldb = C("ldb"); c_ldbb = C("ldbb"); c_ldgh = C("ldgh"); c_ldgt = C("ldgt")
    c_gwq = DmaCtr(nc, "gwq"); c_tq = C("tq"); c_aqe = C("aqe")
    c_ts = C("ts"); c_vnm = C("vnm"); c_aexp = C("aexp")
    c_tpt = C("tpt"); c_vpte = C("vpte"); c_vc = C("vc")
    c_to = C("to"); c_aoe = C("aoe"); c_aod = DmaCtr(nc, "aod")
    c_ldoa = DmaCtr(nc, "ldoa"); c_ldob = DmaCtr(nc, "ldob"); c_tot = C("tot"); c_voe = C("voe")
    c_gwo = DmaCtr(nc, "gwo"); c_twog = C("twog"); c_awoc = C("awoc")
    c_vwos = C("vwos"); c_awod = DmaCtr(nc, "awod")
    c_lda = DmaCtr(nc, "lda"); c_ldxr = DmaCtr(nc, "ldxr")
    c_vagg = C("vagg"); c_ainv3 = C("ainv3"); c_vnmi3 = C("vnmi3")
    c_anorm = C("anorm"); c_vadd = C("vadd"); c_ax1d = DmaCtr(nc, "ax1d")
    c_vbnb = C("vbnb"); c_vbs2 = C("vbs2"); c_aln3 = C("aln3"); c_aln2b = C("aln2b"); c_vgc = C("vgc"); c_ainv2 = C("ainv2"); c_vx2n = C("vx2n")
    c_tx2t = C("tx2t"); c_vx2e = C("vx2e"); c_thg = C("thg")
    c_vg1 = C("vg1"); c_ag2 = C("ag2"); c_vg3 = C("vg3")
    c_tht = C("tht"); c_vhte = C("vhte")
    c_gwb = DmaCtr(nc, "gwb"); c_tfg = C("tfg"); c_vfev = C("vfev"); c_vfst = C("vfst"); c_afd = DmaCtr(nc, "afd")
    c_ldf = DmaCtr(nc, "ldf"); c_ldx1 = DmaCtr(nc, "ldx1")
    c_vagf = C("vagf"); c_aln5 = C("aln5"); c_bar4 = C("bar4"); c_bar5 = C("bar5"); c_ainv5 = C("ainv5"); c_vnmi5 = C("vnmi5")
    c_anorm5 = C("anorm5"); c_vy = C("vy"); c_ayd = DmaCtr(nc, "ayd")

    ALL_CTRS = [v for v in list(locals().values())
                if isinstance(v, (Ctr, DmaCtr))]

    def snapshot():
        out = []
        for c in ALL_CTRS:
            if isinstance(c, DmaCtr):
                out.extend([(c.sems[0], c.vals[0]), (c.sems[1], c.vals[1])])
            else:
                out.append((c.sem, c.val))
        return out

    def barrier(eng, snap):
        for sem_, val_ in snap:
            if val_ > 0:
                eng.wait_ge(sem_, val_)

    iden = nc.alloc_sbuf_tensor("iden", [128, 128], BF16)
    ones = nc.alloc_sbuf_tensor("ones", [128, 128], BF16)
    epsc = nc.alloc_sbuf_tensor("epsc", [128, 1], F32)
    btr = nc.alloc_sbuf_tensor("btr", [128, 1], F32)
    nbtr = nc.alloc_sbuf_tensor("nbtr", [128, 1], F32)
    cols = nc.alloc_sbuf_tensor("cols", [128, 2, 8], F32)
    bn6 = nc.alloc_sbuf_tensor("bn6", [128, 2, 8, 6], F32)
    bn2 = nc.alloc_sbuf_tensor("bn2", [128, 2, 2], F32)
    bn6a = nc.alloc_sbuf_tensor("bn6a", [128, NT_OWN, 8, 6], F32)
    bn6f = nc.alloc_sbuf_tensor("bn6f", [128, NT_OWN, 8, 6], F32)

    ctx = ExitStack()
    kT = ctx.enter_context(nc.sbuf_tensor("kT", [128, NKV * T_ALL], BF16))
    v_sb = ctx.enter_context(nc.sbuf_tensor("v_sb", [128, NT_ALL * 1024], BF16))
    xnT_own = ctx.enter_context(nc.sbuf_tensor("xnT_own", [128, NDC * T_OWN], BF16))
    xnT_halo = ctx.enter_context(nc.sbuf_tensor("xnT_halo", [128, NDC * T_HALO], BF16))

    xnT_own3 = xnT_own[:].rearrange("p (dc t) -> p dc t", dc=NDC)
    xnT_halo3 = xnT_halo[:].rearrange("p (dc t) -> p dc t", dc=NDC)
    v_sb3 = v_sb[:].rearrange("p (mt o) -> p mt o", mt=NT_ALL)

    with (
        nc.sbuf_tensor("x_sb", [128, 2, D], F32) as x_sb,
        nc.sbuf_tensor("xn_bf", [128, D], BF16) as xn_bf,
        nc.sbuf_tensor("wkv_bf", [128, 2, D], BF16) as wkv_bf,
        nc.sbuf_tensor("vT_tmp", [128, 1, T_ALL], BF16) as vT_tmp,
        nc.psum_tensor("ps_t", [128, 2, 512], F32) as ps_t,
        nc.psum_tensor("ps_kv", [128, 2, T_ALL], F32) as ps_kv,
        nc.Block() as b1,
    ):
        def xe_dst(i):
            if i < 4:
                return xnT_halo3, i * 128
            return xnT_own3, (i - 4) * 128

        @b1.sync
        def _(sy):
            sy.dma_start(iden[:], idend).then_inc(c_ldi.sem, 16)
            sy.dma_start(btr[:], btrd).then_inc(c_ldb.sem, 16)
            for i in range(NT_ALL):
                if i >= 2:
                    sy.wait_ge(c_axn.sem, i - 1)
                c_ldx.inc(sy.dma_start(x_sb[:, i % 2, :],
                                       xs[i * 128:(i + 1) * 128, :]), i)
        c_ldi.val += 16
        c_ldb.val += 16

        @b1.vector
        def _(v):
            c_vini.inc(v.memset(epsc[:], 1e-5))
            v.wait_ge(c_ldb.sem, 16)
            c_vini.inc(v.tensor_scalar_mul(nbtr[:], btr[:], -1.0))
            c_vini.inc(v.memset(ones[:], 0.0))
            v.wait_ge(c_vini.sem, 3)
            c_vini.inc(v.memset(ones[0:1, :], 1.0))
            for i in range(NT_ALL):
                p = i % 2
                c_ldx.wait(v, i)
                if i >= 2:
                    v.wait_ge(c_vnmi.sem, i - 1)
                for ch in range(8):
                    c_vbs.inc(v.bn_stats(bn6[:, p, ch, :],
                                         x_sb[:, p, ch * 512:(ch + 1) * 512]))
                v.wait_ge(c_vbs.sem, c_vbs.val)
                c_vbn.inc(v.bn_aggr(bn2[:, p, :], bn6[:, p, :, :]))
                if i >= 1 and sub >= 3:
                    dst3, off = xe_dst(i - 1)
                    for g in range(8):
                        gi = (i - 1) * 8 + g
                        v.wait_ge(c_txt.sem, gi + 1)
                        c_vxe.inc(v.tensor_copy(
                            dst3[:, 4 * g:4 * g + 4, off:off + 128],
                            ps_t[:, gi % 2, :].rearrange("p (c t) -> p c t", c=4)))
                v.wait_ge(c_ainv.sem, i + 1)
                c_vnmi.inc(v.tensor_scalar(cols[:, p, 2:3], bn2[:, p, 0:1],
                                           cols[:, p, 1:2], -1.0,
                                           ALU.mult, ALU.mult))
            if sub >= 3:
                dst3, off = xe_dst(NT_ALL - 1)
                for g in range(8):
                    gi = (NT_ALL - 1) * 8 + g
                    v.wait_ge(c_txt.sem, gi + 1)
                    c_vxe.inc(v.tensor_copy(
                        dst3[:, 4 * g:4 * g + 4, off:off + 128],
                        ps_t[:, gi % 2, :].rearrange("p (c t) -> p c t", c=4)))

        @b1.scalar
        def _(s):
            s.wait_ge(c_vini.sem, 2)
            for i in range(NT_ALL):
                p = i % 2
                s.wait_ge(c_vbn.sem, i + 1)
                c_aln.inc(s.activation(cols[:, p, 0:1], bn2[:, p, 1:2], AF.Ln,
                                       bias=epsc[:], scale=1.0))
                s.wait_ge(c_aln.sem, c_aln.val)
                c_ainv.inc(s.activation(cols[:, p, 1:2], cols[:, p, 0:1], AF.Exp,
                                        bias=0.0, scale=-0.5))
                s.wait_ge(c_vnmi.sem, i + 1)
                if i >= 1 and sub >= 2:
                    s.wait_ge(c_txt.sem, 8 * i)
                c_axn.inc(s.activation(xn_bf[:], x_sb[:, p, :], AF.Identity,
                                       bias=cols[:, p, 2:3], scale=cols[:, p, 1:2]))

        @b1.tensor
        def _(t):
            if sub < 2:
                return
            t.wait_ge(c_ldi.sem, 16)
            for i in range(NT_ALL):
                t.wait_ge(c_axn.sem, i + 1)
                for g in range(8):
                    gi = i * 8 + g
                    if gi >= 2 and sub >= 3:
                        t.wait_ge(c_vxe.sem, gi - 1)
                    for q in range(4):
                        dc = 4 * g + q
                        mi = t.matmul(ps_t[:, gi % 2, q * 128:(q + 1) * 128],
                                      xn_bf[:, dc * 128:(dc + 1) * 128], iden[:],
                                      start=True, stop=True)
                    c_txt.inc(mi)

        @b1.gpsimd
        def _(g):
            if sub < 4:
                return
            for j in range(16):
                if j >= 2 and sub >= 5:
                    g.wait_ge(c_tkv.sem, j - 1)
                src = wkt[j] if j < 8 else wvt[j - 8]
                c_gwkv.inc(g.dma_start(wkv_bf[:, j % 2, :],
                                       src.rearrange("p dc o -> p (dc o)")), j)

        def emit_vtr(t, jv):
            ocv = jv - 8
            t.wait_ge(c_akv.sem, jv + 1)
            for g in range(3):
                gi3 = ocv * 3 + g
                if gi3 >= 2:
                    t.wait_ge(c_vve.sem, gi3 - 1)
                for q in range(4):
                    mt = 4 * g + q
                    mi = t.matmul(ps_t[:, gi3 % 2, q * 128:(q + 1) * 128],
                                  vT_tmp[:, 0, mt * 128:(mt + 1) * 128], iden[:],
                                  start=True, stop=True)
                c_tvt.inc(mi)

        @b1.tensor
        def _(t):
            if sub < 5:
                return
            t.wait_ge(c_vxe.sem, 96)
            for j in range(16):
                c_gwkv.wait(t, j)
                if j >= 2:
                    t.wait_ge(c_akv.sem, j - 1)
                for dc in range(NDC):
                    lhs = wkv_bf[:, j % 2, dc * 128:(dc + 1) * 128]
                    t.matmul(ps_kv[:, j % 2, 0:512],
                             lhs, xnT_halo[:, dc * 512:(dc + 1) * 512],
                             start=(dc == 0), stop=(dc == NDC - 1))
                    t.matmul(ps_kv[:, j % 2, 512:1024],
                             lhs, xnT_own[:, dc * 1024:dc * 1024 + 512],
                             start=(dc == 0), stop=(dc == NDC - 1))
                    mi = t.matmul(ps_kv[:, j % 2, 1024:1536],
                                  lhs, xnT_own[:, dc * 1024 + 512:(dc + 1) * 1024],
                                  start=(dc == 0), stop=(dc == NDC - 1))
                c_tkv.inc(mi)
                if j >= 9 and sub >= 6:
                    emit_vtr(t, j - 1)
            if sub >= 6:
                emit_vtr(t, 15)

        @b1.scalar
        def _(s):
            if sub < 5:
                return
            for j in range(16):
                oc = j % 8
                s.wait_ge(c_tkv.sem, j + 1)
                if j < 8:
                    c_akv.inc(s.activation(kT[:, oc * T_ALL:(oc + 1) * T_ALL],
                                           ps_kv[:, j % 2, :], AF.Identity,
                                           bias=0.0, scale=1.0))
                else:
                    if j >= 9 and sub >= 6:
                        s.wait_ge(c_tvt.sem, 3 * (j - 8))
                    c_akv.inc(s.activation(vT_tmp[:, 0, :],
                                           ps_kv[:, j % 2, :], AF.Identity,
                                           bias=0.0, scale=1.0))

        @b1.vector
        def _(v):
            if sub < 6:
                return
            for gi3 in range(24):
                ocv, g = divmod(gi3, 3)
                v.wait_ge(c_tvt.sem, gi3 + 1)
                c_vve.inc(v.tensor_copy(
                    v_sb3[:, 4 * g:4 * g + 4, ocv * 128:(ocv + 1) * 128],
                    ps_t[:, gi3 % 2, :].rearrange("p (c t) -> p c t", c=4)))

    if phases <= 1:
        if dump and sub >= 6:
            d_kT = nc.dram_tensor("d_kT", [128, NKV * T_ALL], BF16, kind="ExternalOutput").ap()
            d_v = nc.dram_tensor("d_v", [128, NT_ALL * 1024], BF16, kind="ExternalOutput").ap()
            d_xo = nc.dram_tensor("d_xo", [128, NDC * T_OWN], BF16, kind="ExternalOutput").ap()
            d_xh = nc.dram_tensor("d_xh", [128, NDC * T_HALO], BF16, kind="ExternalOutput").ap()
            c_dd = Ctr(nc, "dd")
            with nc.Block() as bd:
                @bd.sync
                def _(sy):
                    sy.wait_ge(c_vve.sem, 24)
                    sy.wait_ge(c_akv.sem, 16)
                    c_dd.inc(sy.dma_start(d_kT, kT[:]), 16)
                    c_dd.inc(sy.dma_start(d_v, v_sb[:]), 16)
                    c_dd.inc(sy.dma_start(d_xo, xnT_own[:]), 16)
                    c_dd.inc(sy.dma_start(d_xh, xnT_halo[:]), 16)
                    sy.wait_ge(c_dd.sem, 64)
        ctx.close()
        nc.finalize()
        return nc

    with (
        nc.sbuf_tensor("msk", [128, NT_OWN * WIN * 128], BF16) as msk,
        nc.sbuf_tensor("wq_bf", [128, 2, D], BF16) as wq_bf,
        nc.sbuf_tensor("qT", [128, 2, T_OWN], BF16) as qT,
        nc.sbuf_tensor("pt", [128, 2, WIN * 128], BF16) as pt,
        nc.sbuf_tensor("ptT", [128, 2, WIN * 128], BF16) as ptT,
        nc.sbuf_tensor("o_sb", [128, 2, NT_OWN * HD], BF16) as o_sb,
        nc.sbuf_tensor("scols", [128, 2, 4], F32) as scols,
        nc.psum_tensor("ps_sa", [128, WIN * 128], F32) as ps_sa,
        nc.psum_tensor("ps_sb", [128, WIN * 128], F32) as ps_sb,
        nc.psum_tensor("ps_pt", [128, WIN * 128], F32) as ps_pt,
        nc.psum_tensor("ps_q", [128, 512], F32) as ps_q,
        nc.psum_tensor("ps_o", [128, 128], F32) as ps_o,
        nc.Block() as b2,
    ):
        @b2.sync
        def _(sy):
            sy.wait_ge(c_vve.sem, 24)
            c_ldm.inc(sy.dma_start(msk[:], mskd.rearrange("p a b c -> p (a b c)")), 16)
            for h in range(NQH):
                sy.wait_ge(c_aoe.sem, NT_OWN * (h + 1))
                c_aod.inc(sy.dma_start(o_s[h].rearrange("qt p dh -> p qt dh"),
                                       o_sb[:, h % 2, :].rearrange(
                                           "p (qt dh) -> p qt dh", qt=NT_OWN)), h)

        @b2.gpsimd
        def _(g):
            g.wait_ge(c_vve.sem, 24)
            for h in range(NQH):
                if h >= 2:
                    g.wait_ge(c_tq.sem, 2 * (h - 1))
                c_gwq.inc(g.dma_start(wq_bf[:, h % 2, :],
                                      wqt[h].rearrange("p dc o -> p (dc o)")), h)

        def pe_s(t, u):
            h, qt = divmod(u, NT_OWN)
            kv = h // 4
            t.wait_ge(c_aqe.sem, 2 * (h + 1))
            if u >= 2:
                t.wait_ge(c_aexp.sem, u - 1)
            if u == 0:
                t.wait_ge(c_ldm.sem, 16)
            pss = ps_sa if u % 2 == 0 else ps_sb
            m0 = kv * T_ALL + qt * 128
            mb = qt * WIN * 128
            lhs = qT[:, h % 2, qt * 128:(qt + 1) * 128]
            t.matmul(pss[:, 0:512], lhs, kT[:, m0:m0 + 512],
                     start=True, stop=False)
            t.matmul(pss[:, 0:512], iden[:], msk[:, mb:mb + 512],
                     start=False, stop=True)
            t.matmul(pss[:, 512:640], lhs, kT[:, m0 + 512:m0 + 640],
                     start=True, stop=False)
            c_ts.inc(t.matmul(pss[:, 512:640], iden[:], msk[:, mb + 512:mb + 640],
                              start=False, stop=True))

        def pe_pt(t, u):
            t.wait_ge(c_aexp.sem, u + 1)
            t.wait_ge(c_vpte.sem, u)
            for j in range(WIN):
                mi = t.matmul(ps_pt[:, j * 128:(j + 1) * 128],
                              pt[:, u % 2, j * 128:(j + 1) * 128], iden[:],
                              start=True, stop=True)
            c_tpt.inc(mi)

        def pe_o(t, u):
            h, qt = divmod(u, NT_OWN)
            kv = h // 4
            t.wait_ge(c_vpte.sem, u + 1)
            if u >= 1:
                t.wait_ge(c_aoe.sem, u)
            pso = ps_o
            for j in range(WIN):
                mt = qt + j
                mi = t.matmul(pso[:, :],
                              ptT[:, u % 2, j * 128:(j + 1) * 128],
                              v_sb[:, mt * 1024 + kv * 128:mt * 1024 + (kv + 1) * 128],
                              start=(j == 0), stop=(j == WIN - 1))
            c_to.inc(mi)

        @b2.tensor
        def _(t):
            for h in range(NQH):
                c_gwq.wait(t, h)
                for tc in range(2):
                    t.wait_ge(c_aqe.sem, 2 * h + tc)
                    for dc in range(NDC):
                        mi = t.matmul(ps_q[:, :],
                                      wq_bf[:, h % 2, dc * 128:(dc + 1) * 128],
                                      xnT_own[:, dc * 1024 + tc * 512:dc * 1024 + (tc + 1) * 512],
                                      start=(dc == 0), stop=(dc == NDC - 1))
                    c_tq.inc(mi)
                for qt in range(NT_OWN):
                    u = h * NT_OWN + qt
                    if u >= 2:
                        pe_o(t, u - 2)
                    if u >= 1:
                        pe_pt(t, u - 1)
                    pe_s(t, u)
            pe_pt(t, NU - 1)
            pe_o(t, NU - 2)
            pe_o(t, NU - 1)

        @b2.vector
        def _(v):
            for u in range(NU):
                if u >= 1:
                    v.wait_ge(c_tpt.sem, u)
                    if u >= 3:
                        v.wait_ge(c_to.sem, u - 2)
                    c_vpte.inc(v.tensor_copy(ptT[:, (u - 1) % 2, :], ps_pt[:]))
                    if u >= 3:
                        v.wait_ge(c_aoe.sem, u - 2)
                    c_vc.inc(v.reciprocal(scols[:, (u - 1) % 2, 2:3],
                                          scols[:, (u - 1) % 2, 1:2]))
                v.wait_ge(c_ts.sem, u + 1)
                if u >= 2:
                    v.wait_ge(c_aexp.sem, u - 1)
                c_vnm.inc(v.reduce_max(scols[:, u % 2, 0:1],
                                       (ps_sa if u % 2 == 0 else ps_sb)[:, 0:640],
                                       axis=AX.X, negate=True))
            v.wait_ge(c_tpt.sem, NU)
            c_vpte.inc(v.tensor_copy(ptT[:, (NU - 1) % 2, :], ps_pt[:]))
            v.wait_ge(c_aoe.sem, NU - 2)
            c_vc.inc(v.reciprocal(scols[:, (NU - 1) % 2, 2:3],
                                  scols[:, (NU - 1) % 2, 1:2]))

        def oevac(s, u):
            h, qt = divmod(u, NT_OWN)
            s.wait_ge(c_to.sem, u + 1)
            s.wait_ge(c_vc.sem, u + 1)
            if qt == 0 and h >= 2:
                c_aod.wait(s, h - 2)
            c_aoe.inc(s.activation(o_sb[:, h % 2, qt * 128:(qt + 1) * 128],
                                   ps_o[:], AF.Identity, bias=0.0,
                                   scale=scols[:, u % 2, 2:3]))


        @b2.scalar
        def _(s):
            for h in range(NQH):
                for tc in range(2):
                    s.wait_ge(c_tq.sem, 2 * h + tc + 1)
                    if h >= 2 and tc == 0:
                        s.wait_ge(c_ts.sem, NT_OWN * (h - 1))
                    c_aqe.inc(s.activation(qT[:, h % 2, tc * 512:(tc + 1) * 512],
                                           ps_q[:], AF.Identity,
                                           bias=0.0, scale=QSCALE))
                for qt in range(NT_OWN):
                    u = h * NT_OWN + qt
                    if u >= 2:
                        oevac(s, u - 2)
                    s.wait_ge(c_vnm.sem, u + 1)
                    if u >= 2:
                        s.wait_ge(c_tpt.sem, u - 1)
                        s.wait_ge(c_vc.sem, u - 1)
                    c_aexp.inc(s.activation(pt[:, u % 2, :],
                                            (ps_sa if u % 2 == 0 else ps_sb)[:, 0:640],
                                            AF.Exp,
                                            bias=scols[:, u % 2, 0:1], scale=1.0,
                                            accum_out=scols[:, u % 2, 1:2]))
            oevac(s, NU - 2)
            oevac(s, NU - 1)

    ctx.close()

    if phases <= 2:
        nc.finalize()
        return nc

    with (
        nc.sbuf_tensor("o_row", [128, 2, D], BF16) as o_row,
        nc.sbuf_tensor("oT", [128, 32 * T_OWN], BF16) as oT,
        nc.sbuf_tensor("wo_bf", [128, 2, 32 * 512], BF16) as wo_bf,
        nc.sbuf_tensor("stg3", [128, 2, 512], F32) as stg3,
    ):
        oT3 = oT[:].rearrange("p (oc t) -> p oc t", oc=32)
        with (
            nc.psum_tensor("ps_t3a", [128, 2, 512], F32) as ps_t3a,
            nc.Block() as b3a,
        ):
            @b3a.sync
            def _(sy):
                c_aod.wait_all(sy, NQH)
                for tt in range(NT_OWN):
                    if tt >= 2:
                        sy.wait_ge(c_tot.sem, 8 * (tt - 1))
                    c_ldoa.inc(sy.dma_start(
                        o_row[:, tt % 2, 0:2048].rearrange("p (h dh) -> p h dh", h=16),
                        o_s[0:16, tt, :, :].rearrange("h p dh -> p h dh")), tt)
                    c_ldob.inc(sy.dma_start(
                        o_row[:, tt % 2, 2048:4096].rearrange("p (h dh) -> p h dh", h=16),
                        o_s[16:32, tt, :, :].rearrange("h p dh -> p h dh")), tt)

            @b3a.tensor
            def _(t):
                for tt in range(NT_OWN):
                    c_ldoa.wait(t, tt)
                    c_ldob.wait(t, tt)
                    for g in range(8):
                        gi = tt * 8 + g
                        if gi >= 2:
                            t.wait_ge(c_voe.sem, gi - 1)
                        for q in range(4):
                            oc = 4 * g + q
                            mi = t.matmul(ps_t3a[:, gi % 2, q * 128:(q + 1) * 128],
                                          o_row[:, tt % 2, oc * 128:(oc + 1) * 128],
                                          iden[:], start=True, stop=True)
                        c_tot.inc(mi)

            @b3a.vector
            def _(v):
                for gi in range(64):
                    tt, g = divmod(gi, 8)
                    v.wait_ge(c_tot.sem, gi + 1)
                    c_voe.inc(v.tensor_copy(
                        oT3[:, 4 * g:4 * g + 4, tt * 128:(tt + 1) * 128],
                        ps_t3a[:, gi % 2, :].rearrange("p (c t) -> p c t", c=4)))

        with (
            nc.psum_tensor("ps_wo", [128, NT_OWN, 512], F32) as ps_wo,
            nc.Block() as b3w,
        ):
            @b3w.gpsimd
            def _(g):
                c_aod.wait_all(g, NQH)
                for db in range(8):
                    if db >= 2:
                        g.wait_ge(c_twog.sem, 8 * (db - 1))
                    c_gwo.inc(g.dma_start(
                        wo_bf[:, db % 2, :].rearrange("p (oc dd) -> p oc dd", oc=32),
                        wot[db].rearrange("oc p dd -> p oc dd")), db)

            @b3w.tensor
            def _(t):
                t.wait_ge(c_voe.sem, 64)
                for db in range(8):
                    c_gwo.wait(t, db)
                    for tt in range(NT_OWN):
                        if db >= 1:
                            t.wait_ge(c_awoc.sem, 8 * (db - 1) + tt + 1)
                        for oc in range(32):
                            mi = t.matmul(ps_wo[:, tt, :],
                                          oT3[:, oc, tt * 128:(tt + 1) * 128],
                                          wo_bf[:, db % 2, oc * 512:(oc + 1) * 512],
                                          start=(oc == 0), stop=(oc == 31))
                        c_twog.inc(mi)

            @b3w.scalar
            def _(s):
                for k in range(64):
                    db, tt = divmod(k, 8)
                    s.wait_ge(c_twog.sem, k + 1)
                    if k >= 2:
                        s.wait_ge(c_vwos.sem, k - 1)
                        c_awod.wait(s, k - 2)
                    c_awoc.inc(s.activation(stg3[:, k % 2, :], ps_wo[:, tt, :],
                                            AF.Identity, bias=0.0, scale=1.0))

            @b3w.sync
            def _(sy):
                for k in range(64):
                    db, tt = divmod(k, 8)
                    sy.wait_ge(c_awoc.sem, k + 1)
                    c_awod.inc(sy.dma_start(
                        attn_s[tt * 128:(tt + 1) * 128, db * 512:(db + 1) * 512],
                        stg3[:, k % 2, :]), k)

            @b3w.vector
            def _(v):
                for k in range(64):
                    db, tt = divmod(k, 8)
                    v.wait_ge(c_awoc.sem, k + 1)
                    c_vwos.inc(v.bn_stats(bn6a[:, tt, db, :], stg3[:, k % 2, :]))

    if phases <= 3:
        nc.finalize()
        return nc

    s5a = ExitStack()
    x2nT = s5a.enter_context(nc.sbuf_tensor("x2nT", [128, NDC * T_OWN], BF16))
    hpT = s5a.enter_context(nc.sbuf_tensor("hpT", [128, NT_OWN * 128], BF16))
    hcat = s5a.enter_context(nc.sbuf_tensor("hcat", [128, NDC * 137], BF16))
    x2nT3 = x2nT[:].rearrange("p (dc t) -> p dc t", dc=NDC)

    P3W_DONE = 16 * 64

    with (
        nc.sbuf_tensor("a_row", [128, 2, D], F32) as a_row,
        nc.sbuf_tensor("x_row", [128, 2, D], F32) as x_row,
        nc.sbuf_tensor("lt_row", [128, D], F32) as lt_row,
        nc.sbuf_tensor("x1_row", [128, 2, D], F32) as x1_row,
        nc.sbuf_tensor("x2n_bf", [128, D], BF16) as x2n_bf,
        nc.sbuf_tensor("bn6b", [128, 2, 8, 6], F32) as bn6b,
        nc.sbuf_tensor("bn2b", [128, 2, 4], F32) as bn2b,
        nc.sbuf_tensor("cols3", [128, 2, 8], F32) as cols3,
        nc.sbuf_tensor("g8", [128, 8], F32) as g8,
        nc.sbuf_tensor("gcols", [128, 8], F32) as gcols,
        nc.sbuf_tensor("hp_bf", [128, 128], BF16) as hp_bf,
        nc.psum_tensor("ps_t3b", [128, 2, 512], F32) as ps_t3b,
        nc.psum_tensor("ps_h", [128, 4, 512], F32) as ps_h,
        nc.Block() as b4,
    ):
        def hslice(tt):
            bank, reg = divmod(tt, 2)
            return ps_h[:, bank, reg * 256:reg * 256 + 137]

        @b4.sync
        def _(sy):
            c_awod.wait_all(sy, 64)
            for tt in range(NT_OWN):
                if tt >= 2:
                    sy.wait_ge(c_anorm.sem, tt - 1)
                c_lda.inc(sy.dma_start(a_row[:, tt % 2, :],
                                       attn_s[tt * 128:(tt + 1) * 128, :]), tt)
                if tt >= 2:
                    sy.wait_ge(c_vadd.sem, tt - 1)
                c_ldxr.inc(sy.dma_start(
                    x_row[:, tt % 2, :],
                    xs[T_HALO + tt * 128:T_HALO + (tt + 1) * 128, :]), tt)
                sy.wait_ge(c_vadd.sem, tt + 1)
                c_ax1d.inc(sy.dma_start(x1_s[tt * 128:(tt + 1) * 128, :],
                                        x1_row[:, tt % 2, :]), tt)

        @b4.gpsimd
        def _(g):
            g.wait_ge(c_twog.sem, 64)
            c_awod.wait_all(g, 64)
            c_ldgh.inc(g.dma_start(
                hcat[:].rearrange("p (dc j) -> p dc j", dc=32),
                hct.rearrange("dc p j -> p dc j")), 16)
            for tt in range(NT_OWN):
                p = tt % 2
                g.wait_ge(c_anorm.sem, tt + 1)
                c_ldxr.wait(g, tt)
                if tt >= 2:
                    g.wait_ge(c_vx2n.sem, tt - 1)
                    c_ax1d.wait(g, tt - 2)
                c_vadd.inc(g.tensor_add(x1_row[:, p, :], x_row[:, p, :], lt_row[:]))

        def v_gating(v, tt):
            hs = hslice(tt)

            def step(inst):
                c_vgc.inc(inst)
                v.wait_ge(c_vgc.sem, c_vgc.val)

            v.wait_ge(c_thg.sem, tt + 2 if tt % 2 == 0 else tt + 1)
            c_vg1.inc(v.reduce_max(gcols[:, 0:1], hs[:, 128:136],
                                   axis=AX.X, negate=True))
            v.wait_ge(c_ag2.sem, 2 * (tt + 1))
            step(v.reciprocal(gcols[:, 2:3], gcols[:, 1:2]))
            step(v.tensor_scalar_add(gcols[:, 4:5], gcols[:, 3:4], 1.0))
            step(v.reciprocal(gcols[:, 5:6], gcols[:, 4:5]))
            step(v.tensor_scalar_mul(gcols[:, 6:7], gcols[:, 5:6], -0.125))
            step(v.tensor_scalar(g8[:], g8[:], gcols[:, 2:3], gcols[:, 6:7],
                                 ALU.mult, ALU.add))
            step(v.tensor_scalar_max(g8[:], g8[:], 0.0))
            step(v.reduce_sum(gcols[:, 7:8], g8[:], axis=AX.X))
            step(v.tensor_single_scalar(gcols[:, 0:1], gcols[:, 7:8], 0.0,
                                        ALU.is_equal))
            step(v.tensor_add(gcols[:, 1:2], gcols[:, 7:8], gcols[:, 0:1]))
            step(v.reciprocal(gcols[:, 4:5], gcols[:, 1:2]))
            step(v.tensor_scalar_mul(gcols[:, 5:6], gcols[:, 4:5], 2.0))
            step(v.tensor_scalar(g8[:], g8[:], gcols[:, 5:6], None, ALU.mult))
            if tt >= 1:
                v.wait_ge(c_tht.sem, tt)
            for e in range(8):
                mi = v.tensor_scalar(hp_bf[:, e * 16:(e + 1) * 16],
                                     hs[:, e * 16:(e + 1) * 16],
                                     g8[:, e:e + 1], None, ALU.mult)
                if e < 7:
                    c_vgc.inc(mi)
                else:
                    c_vg3.inc(mi)
            v.wait_ge(c_vgc.sem, c_vgc.val)
            v.wait_ge(c_tht.sem, tt + 1)
            c_vhte.inc(v.tensor_copy(hpT[:, tt * 128:(tt + 1) * 128],
                                     ps_t3b[:, tt % 2, 0:128]))

        @b4.vector
        def _(v):
            v.wait_ge(c_twog.sem, 64)
            c_awod.wait_all(v, 64)
            for tt in range(NT_OWN):
                p = tt % 2
                v.wait_ge(c_vwos.sem, 56 + tt + 1)
                if tt >= 2:
                    v.wait_ge(c_anorm.sem, tt - 1)
                c_vagg.inc(v.bn_aggr(cols3[:, p, 4:6], bn6a[:, tt, :, :]))
                v.wait_ge(c_ainv3.sem, tt + 1)
                c_vnmi3.inc(v.tensor_scalar(cols3[:, p, 2:3], cols3[:, p, 4:5],
                                            cols3[:, p, 1:2], -1.0,
                                            ALU.mult, ALU.mult))
                v.wait_ge(c_vadd.sem, tt + 1)
                for ch in range(8):
                    c_vbs2.inc(v.bn_stats(bn6b[:, p, ch, :],
                                          x1_row[:, p, ch * 512:(ch + 1) * 512]))
                v.wait_ge(c_vbs2.sem, c_vbs2.val)
                c_vbnb.inc(v.bn_aggr(bn2b[:, p, 0:2], bn6b[:, p, :, :]))
                v.wait_ge(c_ainv2.sem, tt + 1)
                if tt >= 1:
                    v.wait_ge(c_tx2t.sem, 8 * tt)
                c_vx2n.inc(v.tensor_scalar(x2n_bf[:], x1_row[:, p, :],
                                           bn2b[:, p, 0:1], bn2b[:, p, 3:4],
                                           ALU.subtract, ALU.mult))
                for g in range(8):
                    gi = tt * 8 + g
                    v.wait_ge(c_tx2t.sem, gi + 1)
                    c_vx2e.inc(v.tensor_copy(
                        x2nT3[:, 4 * g:4 * g + 4, tt * 128:(tt + 1) * 128],
                        ps_t3b[:, gi % 2, :].rearrange("p (c t) -> p c t", c=4)))
                if tt >= 1:
                    v_gating(v, tt - 1)
            v_gating(v, NT_OWN - 1)

        @b4.scalar
        def _(s):
            c_awod.wait_all(s, 64)
            s.wait_ge(c_vini.sem, 2)
            for tt in range(NT_OWN):
                p = tt % 2
                s.wait_ge(c_vagg.sem, tt + 1)
                c_aln3.inc(s.activation(cols3[:, p, 0:1], cols3[:, p, 5:6], AF.Ln,
                                        bias=epsc[:], scale=1.0))
                s.wait_ge(c_aln3.sem, c_aln3.val)
                c_ainv3.inc(s.activation(cols3[:, p, 1:2], cols3[:, p, 0:1],
                                         AF.Exp, bias=0.0, scale=-0.5))
                s.wait_ge(c_vnmi3.sem, tt + 1)
                c_lda.wait(s, tt)
                if tt >= 1:
                    s.wait_ge(c_vadd.sem, tt)
                c_anorm.inc(s.activation(lt_row[:], a_row[:, p, :], AF.Identity,
                                         bias=cols3[:, p, 2:3],
                                         scale=cols3[:, p, 1:2]))

                s.wait_ge(c_vbnb.sem, tt + 1)
                c_aln2b.inc(s.activation(bn2b[:, p, 2:3], bn2b[:, p, 1:2], AF.Ln,
                                         bias=epsc[:], scale=1.0))
                s.wait_ge(c_aln2b.sem, c_aln2b.val)
                c_ainv2.inc(s.activation(bn2b[:, p, 3:4], bn2b[:, p, 2:3],
                                         AF.Exp, bias=0.0, scale=-0.5))
                if tt >= 1:
                    ttp = tt - 1
                    hs = hslice(ttp)
                    s.wait_ge(c_vg1.sem, ttp + 1)
                    s.wait_ge(c_vg3.sem, ttp)
                    s.activation(g8[:], hs[:, 128:136], AF.Exp,
                                 bias=gcols[:, 0:1], scale=1.0,
                                 accum_out=gcols[:, 1:2])
                    c_ag2.inc(s.activation(gcols[:, 3:4], hs[:, 136:137], AF.Exp,
                                           bias=nbtr[:], scale=-1.0), 2)
            ttp = NT_OWN - 1
            hs = hslice(ttp)
            s.wait_ge(c_vg1.sem, ttp + 1)
            s.wait_ge(c_vg3.sem, ttp)
            s.activation(g8[:], hs[:, 128:136], AF.Exp,
                         bias=gcols[:, 0:1], scale=1.0,
                         accum_out=gcols[:, 1:2])
            c_ag2.inc(s.activation(gcols[:, 3:4], hs[:, 136:137], AF.Exp,
                                   bias=nbtr[:], scale=-1.0), 2)

        def emit_hptr(t, tt):
            t.wait_ge(c_vg3.sem, tt + 1)
            if tt >= 2:
                t.wait_ge(c_vhte.sem, tt - 1)
            c_tht.inc(t.matmul(ps_t3b[:, tt % 2, 0:128], hp_bf[:], iden[:],
                               start=True, stop=True))

        @b4.tensor
        def _(t):
            t.wait_ge(c_ldi.sem, 16)
            t.wait_ge(c_ldgh.sem, 16)
            for tt in range(NT_OWN):
                t.wait_ge(c_vx2n.sem, tt + 1)
                for g in range(8):
                    gi = tt * 8 + g
                    if gi >= 2:
                        t.wait_ge(c_vx2e.sem, gi - 1)
                    for q in range(4):
                        dc = 4 * g + q
                        mi = t.matmul(ps_t3b[:, gi % 2, q * 128:(q + 1) * 128],
                                      x2n_bf[:, dc * 128:(dc + 1) * 128], iden[:],
                                      start=True, stop=True)
                    c_tx2t.inc(mi)
                t.wait_ge(c_vx2e.sem, 8 * (tt + 1))
                for dc in range(NDC):
                    mi = t.matmul(hslice(tt),
                                  x2nT3[:, dc, tt * 128:(tt + 1) * 128],
                                  hcat[:, dc * 137:(dc + 1) * 137],
                                  start=(dc == 0), stop=(dc == NDC - 1))
                c_thg.inc(mi)
                if tt >= 1:
                    emit_hptr(t, tt - 1)
            emit_hptr(t, NT_OWN - 1)

    if phases <= 4:
        if 300 <= sub < 400:
            extra = [nc.alloc_semaphore(f"pad{i}") for i in range(sub - 300)]
            with nc.Block() as bps:
                @bps.sync
                def _(sy):
                    for e in extra:
                        sy.nop().then_inc(e, 1)
                        sy.wait_ge(e, 1)
        if sub in (62, 63, 64):
            with nc.Block() as bpad:
                if sub == 62:
                    @bpad.vector
                    def _(v):
                        for _ in range(400):
                            v.engine_nop()
                elif sub == 63:
                    @bpad.tensor
                    def _(t):
                        for _ in range(2200):
                            t.nop()
                elif sub >= 64:
                    @bpad.scalar
                    def _(srr):
                        for _ in range(sub - 64):
                            srr.engine_nop()
        s5a.close()
        nc.finalize()
        return nc

    snap4 = snapshot()
    with (
        nc.sbuf_tensor("wb_bf", [128, 2, 32 * 512], BF16) as wb_bf,
        nc.sbuf_tensor("bT_bf", [128, 4096], BF16) as bT_bf,
        nc.sbuf_tensor("bb", [128, 4096], BF16) as bb,
        nc.sbuf_tensor("stg4", [128, 2, 512], F32) as stg4,
        nc.psum_tensor("ps_f", [128, NT_OWN, 512], F32) as ps_f,
        nc.Block() as b5,
    ):
        @b5.gpsimd
        def _(g):
            g.wait_ge(c_bar4.sem, 1)
            if sub == 70:
                return
            c_ldgt.inc(g.dma_start(bT_bf[:], bTd), 16)
            c_ldgt.inc(g.dma_start(bb[:], bbd), 16)
            for ob in range(8):
                if ob >= 2 and sub >= 72:
                    g.wait_ge(c_tfg.sem, 8 * (ob - 1))
                c_gwb.inc(g.dma_start(
                    wb_bf[:, ob % 2, :].rearrange("p (dc oo) -> p dc oo", dc=32),
                    wbt[ob].rearrange("dc p oo -> p dc oo")), ob)
                if sub == 71:
                    c_gwb.wait(g, ob)

        @b5.sync
        def _(sy):
            barrier(sy, snap4)
            c_bar4.inc(sy.nop())

        @b5.tensor
        def _(t):
            if sub in (70, 71):
                return
            t.wait_ge(c_bar4.sem, 1)
            t.wait_ge(c_ldgt.sem, 32)
            t.wait_ge(c_ldgh.sem, 16)
            t.wait_ge(c_vini.sem, 4)
            for ob in range(8):
                c_gwb.wait(t, ob)
                for tt in range(NT_OWN):
                    if ob >= 1:
                        t.wait_ge(c_vfev.sem, 8 * (ob - 1) + tt + 1)
                    for dc in range(NDC):
                        t.matmul(ps_f[:, tt, :],
                                 x2nT3[:, dc, tt * 128:(tt + 1) * 128],
                                 wb_bf[:, ob % 2, dc * 512:(dc + 1) * 512],
                                 start=(dc == 0), stop=False)
                    t.matmul(ps_f[:, tt, :],
                             hpT[:, tt * 128:(tt + 1) * 128],
                             bT_bf[:, ob * 512:(ob + 1) * 512],
                             start=False, stop=False)
                    mi = t.matmul(ps_f[:, tt, :], ones[0:1, :],
                                  bb[0:1, ob * 512:(ob + 1) * 512],
                                  start=False, stop=True)
                    c_tfg.inc(mi)

        @b5.scalar
        def _(s):
            if sub in (70, 71, 72):
                return
            s.wait_ge(c_bar4.sem, 1)
            for k in range(64):
                ob, tt = divmod(k, 8)
                s.wait_ge(c_tfg.sem, k + 1)
                if k >= 2:
                    c_afd.wait(s, k - 2)
                    s.wait_ge(c_vfst.sem, k - 1)
                c_vfev.inc(s.activation(stg4[:, k % 2, :], ps_f[:, tt, :],
                                        AF.Identity, bias=0.0, scale=1.0))

        @b5.vector
        def _(v):
            if sub in (70, 71, 72, 79):
                return
            v.wait_ge(c_bar4.sem, 1)
            for k in range(64):
                ob, tt = divmod(k, 8)
                v.wait_ge(c_vfev.sem, k + 1)
                c_vfst.inc(v.bn_stats(bn6f[:, tt, ob, :], stg4[:, k % 2, :]))

        @b5.sync
        def _(sy2):
            if sub in (70, 71, 72, 73, 78, 79):
                return
            for k in range(64):
                ob, tt = divmod(k, 8)
                sy2.wait_ge(c_vfev.sem, k + 1)
                c_afd.inc(sy2.dma_start(
                    ffn_s[tt * 128:(tt + 1) * 128, ob * 512:(ob + 1) * 512],
                    stg4[:, k % 2, :]), k)

    s5a.close()

    P4B_DONE = 16 * 64

    if phases <= 5:
        nc.finalize()
        return nc

    snap5 = snapshot()
    with (
        nc.sbuf_tensor("f_row", [128, 2, D], F32) as f_row,
        nc.sbuf_tensor("x1r", [128, 2, D], F32) as x1r,
        nc.sbuf_tensor("lt5", [128, D], F32) as lt5,
        nc.sbuf_tensor("y_row", [128, 2, D], F32) as y_row,
        nc.sbuf_tensor("cols5", [128, 2, 8], F32) as cols5,
        nc.Block() as b6,
    ):
        @b6.sync
        def _(sy):
            barrier(sy, snap5)
            c_bar5.inc(sy.nop())
            for tt in range(NT_OWN):
                if tt >= 2:
                    sy.wait_ge(c_anorm5.sem, tt - 1)
                c_ldf.inc(sy.dma_start(f_row[:, tt % 2, :],
                                       ffn_s[tt * 128:(tt + 1) * 128, :]), tt)
                if tt >= 2:
                    sy.wait_ge(c_vy.sem, tt - 1)
                c_ldx1.inc(sy.dma_start(x1r[:, tt % 2, :],
                                        x1_s[tt * 128:(tt + 1) * 128, :]), tt)
                if tt >= 2:
                    sy.wait_ge(c_vy.sem, tt - 1)
                    c_ayd.inc(sy.dma_start(y[(tt - 2) * 128:(tt - 1) * 128, :],
                                           y_row[:, tt % 2, :]), tt - 2)
            for tt in (NT_OWN - 2, NT_OWN - 1):
                sy.wait_ge(c_vy.sem, tt + 1)
                c_ayd.inc(sy.dma_start(y[tt * 128:(tt + 1) * 128, :],
                                       y_row[:, tt % 2, :]), tt)
            c_ayd.wait_all(sy, NT_OWN)

        @b6.gpsimd
        def _(g):
            g.wait_ge(c_bar5.sem, 1)
            for tt in range(NT_OWN):
                p = tt % 2
                g.wait_ge(c_anorm5.sem, tt + 1)
                c_ldx1.wait(g, tt)
                if tt >= 2:
                    c_ayd.wait(g, tt - 2)
                c_vy.inc(g.tensor_add(y_row[:, p, :], x1r[:, p, :], lt5[:]))

        @b6.vector
        def _(v):
            v.wait_ge(c_bar5.sem, 1)
            for tt in range(NT_OWN):
                p = tt % 2
                if tt >= 2:
                    v.wait_ge(c_anorm5.sem, tt - 1)
                c_vagf.inc(v.bn_aggr(cols5[:, p, 4:6], bn6f[:, tt, :, :]))
                v.wait_ge(c_ainv5.sem, tt + 1)
                c_vnmi5.inc(v.tensor_scalar(cols5[:, p, 2:3], cols5[:, p, 4:5],
                                            cols5[:, p, 1:2], -1.0,
                                            ALU.mult, ALU.mult))


        @b6.scalar
        def _(s):
            s.wait_ge(c_bar5.sem, 1)
            for tt in range(NT_OWN):
                p = tt % 2
                s.wait_ge(c_vagf.sem, tt + 1)
                c_aln5.inc(s.activation(cols5[:, p, 0:1], cols5[:, p, 5:6], AF.Ln,
                                        bias=epsc[:], scale=1.0))
                s.wait_ge(c_aln5.sem, c_aln5.val)
                c_ainv5.inc(s.activation(cols5[:, p, 1:2], cols5[:, p, 0:1],
                                         AF.Exp, bias=0.0, scale=-0.5))
                s.wait_ge(c_vnmi5.sem, tt + 1)
                c_ldf.wait(s, tt)
                if tt >= 1:
                    s.wait_ge(c_vy.sem, tt)
                c_anorm5.inc(s.activation(lt5[:], f_row[:, p, :], AF.Identity,
                                          bias=cols5[:, p, 2:3],
                                          scale=cols5[:, p, 1:2]))


    nc.finalize()
    return nc


def _host_layouts(L0_attn_x, L0_attn_wq, L0_attn_wk, L0_attn_wv, L0_attn_wo,
                  L0_ffn_W_base, L0_ffn_b_base, L0_ffn_W_gate, L0_ffn_W_thr,
                  L0_ffn_b_thr, L0_ffn_A_w, L0_ffn_B_w):
    f32 = np.float32
    x2d = np.ascontiguousarray(np.asarray(L0_attn_x, dtype=f32).reshape(8192, D))
    wq = np.asarray(L0_attn_wq, dtype=f32)
    wk = np.asarray(L0_attn_wk, dtype=f32)
    wv = np.asarray(L0_attn_wv, dtype=f32)
    wo = np.asarray(L0_attn_wo, dtype=f32)
    wb = np.asarray(L0_ffn_W_base, dtype=f32)
    b_base = np.asarray(L0_ffn_b_base, dtype=f32)
    wg = np.asarray(L0_ffn_W_gate, dtype=f32)
    wt = np.asarray(L0_ffn_W_thr, dtype=f32)
    b_thr = np.asarray(L0_ffn_b_thr, dtype=f32)
    A_w = np.asarray(L0_ffn_A_w, dtype=f32)
    B_w = np.asarray(L0_ffn_B_w, dtype=f32)

    wqt = np.ascontiguousarray(wq.T.reshape(32, 128, 32, 128).transpose(2, 1, 0, 3))
    wkt = np.ascontiguousarray(wk.T.reshape(32, 128, 8, 128).transpose(2, 1, 0, 3))
    wvt = np.ascontiguousarray(wv.T.reshape(32, 128, 8, 128).transpose(2, 1, 0, 3))
    wot = np.ascontiguousarray(wo.T.reshape(32, 128, 8, 512).transpose(2, 0, 1, 3))
    wbt = np.ascontiguousarray(wb.T.reshape(32, 128, 8, 512).transpose(2, 0, 1, 3))
    G = np.concatenate([A_w.reshape(128, D), wg, wt[None, :]], axis=0)
    hct = np.ascontiguousarray(G.T.reshape(32, 128, 137))
    bTd = np.ascontiguousarray(B_w.transpose(0, 2, 1).reshape(128, D))
    bbd = np.ascontiguousarray(np.broadcast_to(b_base[None, :], (128, D)))
    btrd = np.ascontiguousarray(np.broadcast_to(b_thr.reshape(1, 1), (128, 1)))
    idend = np.eye(128, dtype=ml_dtypes.bfloat16)

    shared = dict(wqt=wqt, wkt=wkt, wvt=wvt, wot=wot, wbt=wbt, hct=hct,
                  bTd=bTd, bbd=bbd, btrd=btrd, idend=idend)

    ti = np.arange(128)
    in_maps = []
    for c in range(8):
        if c == 0:
            xsl = np.concatenate(
                [np.zeros((T_HALO, D), dtype=f32), x2d[0:T_OWN]], axis=0)
        else:
            xsl = np.ascontiguousarray(
                x2d[c * T_OWN - T_HALO:c * T_OWN + T_OWN])
        m = np.zeros((128, NT_OWN, WIN, 128), dtype=np.float32)
        for qt in range(NT_OWN):
            gq = c * NT_OWN + qt
            tq = gq * 128 + ti[:, None]
            for r in range(WIN):
                mg = (gq + r - 4) * 128 + ti[None, :]
                valid = (tq - mg >= 0) & (tq - mg < 512) & (mg >= 0)
                m[:, qt, r, :] = np.where(valid, 0.0, -1e30)
        im = dict(shared)
        im["xs"] = xsl
        im["mskd"] = m.astype(ml_dtypes.bfloat16)
        in_maps.append(im)
    return in_maps


_NC_CACHE = {}


def kernel(**inputs):
    in_maps = _host_layouts(**inputs)
    if "nc" not in _NC_CACHE:
        _NC_CACHE["nc"] = build_nc()
    nc = _NC_CACHE["nc"]
    res = run_bass_kernel_spmd(nc, in_maps, core_ids=list(range(8)))
    out = np.concatenate([r["y"] for r in res.results], axis=0)
    return out.reshape(1, 8192, D)


if __name__ == "__main__":
    nc = build_nc()
    print("built ok, instructions:", nc.n_instructions())


# version 10 (final)
# speedup vs baseline: 1.1910x; 1.1910x over previous
import numpy as np
import ml_dtypes
from contextlib import ExitStack
import concourse.bass as bass
import concourse.mybir as mybir
from concourse.bass_utils import run_bass_kernel_spmd

F32 = mybir.dt.float32
BF16 = mybir.dt.bfloat16
AF = mybir.ActivationFunctionType
ALU = mybir.AluOpType
AX = mybir.AxisListType

D = 4096
NDC = 32
T_OWN = 1024
T_HALO = 512
T_ALL = 1536
NT_ALL = 12
NT_OWN = 8
NQH = 32
NKV = 8
HD = 128
WIN = 5
NU = NQH * NT_OWN
QSCALE = float(1.0 / np.sqrt(128.0))


class DmaCtr:

    def __init__(self, nc, name):
        self.sems = [nc.alloc_semaphore(name + "_0"), nc.alloc_semaphore(name + "_1")]
        self.vals = [0, 0]

    def inc(self, inst, i):
        inst.then_inc(self.sems[i % 2], 16)
        self.vals[i % 2] += 16

    def wait(self, eng, i):
        if i < 0:
            return
        eng.wait_ge(self.sems[i % 2], 16 * (i // 2 + 1))

    def wait_all(self, eng, n):
        eng.wait_ge(self.sems[0], 16 * ((n + 1) // 2))
        if n > 1:
            eng.wait_ge(self.sems[1], 16 * (n // 2))


class Ctr:
    def __init__(self, nc, name):
        self.sem = nc.alloc_semaphore(name)
        self.val = 0

    def inc(self, inst, n=1):
        inst.then_inc(self.sem, n)
        self.val += n
        return self.val


def build_nc(phases=6, dump=False, sub=99, dumpscratch=False):
    nc = bass.Bass("TRN2", target_bir_lowering=False,
                   dynamic_dma_scratch_size=16384)

    xs = nc.dram_tensor("xs", [T_ALL, D], F32, kind="ExternalInput").ap()
    wqt = nc.dram_tensor("wqt", [NQH, 128, NDC, 128], F32, kind="ExternalInput").ap()
    wkt = nc.dram_tensor("wkt", [NKV, 128, NDC, 128], F32, kind="ExternalInput").ap()
    wvt = nc.dram_tensor("wvt", [NKV, 128, NDC, 128], F32, kind="ExternalInput").ap()
    wot = nc.dram_tensor("wot", [8, 32, 128, 512], F32, kind="ExternalInput").ap()
    wbt = nc.dram_tensor("wbt", [8, NDC, 128, 512], F32, kind="ExternalInput").ap()
    hct = nc.dram_tensor("hct", [NDC, 128, 137], F32, kind="ExternalInput").ap()
    bTd = nc.dram_tensor("bTd", [128, 4096], F32, kind="ExternalInput").ap()
    bbd = nc.dram_tensor("bbd", [128, 4096], F32, kind="ExternalInput").ap()
    btrd = nc.dram_tensor("btrd", [128, 1], F32, kind="ExternalInput").ap()
    mskd = nc.dram_tensor("mskd", [128, NT_OWN, WIN, 128], BF16, kind="ExternalInput").ap()
    idend = nc.dram_tensor("idend", [128, 128], BF16, kind="ExternalInput").ap()
    y = nc.dram_tensor("y", [T_OWN, D], F32, kind="ExternalOutput").ap()
    _sk = "ExternalOutput" if dumpscratch else "Internal"
    o_s = nc.dram_tensor("o_s", [NQH, NT_OWN, 128, HD], BF16, kind=_sk).ap()
    attn_s = nc.dram_tensor("attn_s", [T_OWN, D], F32, kind=_sk).ap()
    x1_s = nc.dram_tensor("x1_s", [T_OWN, D], F32, kind=_sk).ap()
    ffn_s = nc.dram_tensor("ffn_s", [T_OWN, D], F32, kind=_sk).ap()

    C = lambda n: Ctr(nc, n)
    c_ldx = DmaCtr(nc, "ldx"); c_vbs = C("vbs"); c_aln = C("aln"); c_vini = C("vini"); c_vbn = C("vbn"); c_ainv = C("ainv"); c_vnmi = C("vnmi")
    c_axn = C("axn"); c_txt = C("txt"); c_vxe = C("vxe")
    c_gwkv = DmaCtr(nc, "gwkv"); c_tkv = C("tkv"); c_akv = C("akv")
    c_tvt = C("tvt"); c_vve = C("vve")
    c_ldm = C("ldm"); c_ldi = C("ldi"); c_ldb = C("ldb"); c_ldbb = C("ldbb"); c_ldgh = C("ldgh"); c_ldgt = C("ldgt")
    c_gwq = DmaCtr(nc, "gwq"); c_tq = C("tq"); c_aqe = C("aqe")
    c_ts = C("ts"); c_vnm = C("vnm"); c_aexp = C("aexp")
    c_tpt = C("tpt"); c_vpte = C("vpte"); c_vc = C("vc")
    c_to = C("to"); c_aoe = C("aoe"); c_aod = DmaCtr(nc, "aod")
    c_ldoa = DmaCtr(nc, "ldoa"); c_ldob = DmaCtr(nc, "ldob"); c_tot = C("tot"); c_voe = C("voe")
    c_gwo = DmaCtr(nc, "gwo"); c_twog = C("twog"); c_awoc = C("awoc")
    c_vwos = C("vwos"); c_awod = DmaCtr(nc, "awod")
    c_lda = DmaCtr(nc, "lda"); c_ldxr = DmaCtr(nc, "ldxr")
    c_vagg = C("vagg"); c_ainv3 = C("ainv3"); c_vnmi3 = C("vnmi3")
    c_anorm = C("anorm"); c_vadd = C("vadd"); c_ax1d = DmaCtr(nc, "ax1d")
    c_vbnb = C("vbnb"); c_vbs2 = C("vbs2"); c_aln3 = C("aln3"); c_aln2b = C("aln2b"); c_vgc = C("vgc"); c_ainv2 = C("ainv2"); c_vx2n = C("vx2n")
    c_tx2t = C("tx2t"); c_vx2e = C("vx2e"); c_thg = C("thg")
    c_vg1 = C("vg1"); c_ag2 = C("ag2"); c_vg3 = C("vg3")
    c_tht = C("tht"); c_vhte = C("vhte")
    c_gwb = DmaCtr(nc, "gwb"); c_tfg = C("tfg"); c_vfev = C("vfev"); c_vfst = C("vfst"); c_afd = DmaCtr(nc, "afd")
    c_ldf = DmaCtr(nc, "ldf"); c_ldx1 = DmaCtr(nc, "ldx1")
    c_vagf = C("vagf"); c_aln5 = C("aln5"); c_bar4 = C("bar4"); c_bar5 = C("bar5"); c_ainv5 = C("ainv5"); c_vnmi5 = C("vnmi5")
    c_anorm5 = C("anorm5"); c_vy = C("vy"); c_ayd = DmaCtr(nc, "ayd")

    ALL_CTRS = [v for v in list(locals().values())
                if isinstance(v, (Ctr, DmaCtr))]

    def snapshot():
        out = []
        for c in ALL_CTRS:
            if isinstance(c, DmaCtr):
                out.extend([(c.sems[0], c.vals[0]), (c.sems[1], c.vals[1])])
            else:
                out.append((c.sem, c.val))
        return out

    def barrier(eng, snap):
        for sem_, val_ in snap:
            if val_ > 0:
                eng.wait_ge(sem_, val_)

    iden = nc.alloc_sbuf_tensor("iden", [128, 128], BF16)
    ones = nc.alloc_sbuf_tensor("ones", [128, 128], BF16)
    epsc = nc.alloc_sbuf_tensor("epsc", [128, 1], F32)
    btr = nc.alloc_sbuf_tensor("btr", [128, 1], F32)
    nbtr = nc.alloc_sbuf_tensor("nbtr", [128, 1], F32)
    cols = nc.alloc_sbuf_tensor("cols", [128, 2, 8], F32)
    bn6 = nc.alloc_sbuf_tensor("bn6", [128, 2, 8, 6], F32)
    bn2 = nc.alloc_sbuf_tensor("bn2", [128, 2, 2], F32)
    bn6a = nc.alloc_sbuf_tensor("bn6a", [128, NT_OWN, 8, 6], F32)
    bn6f = nc.alloc_sbuf_tensor("bn6f", [128, NT_OWN, 8, 6], F32)

    ctx = ExitStack()
    kT = ctx.enter_context(nc.sbuf_tensor("kT", [128, NKV * T_ALL], BF16))
    v_sb = ctx.enter_context(nc.sbuf_tensor("v_sb", [128, NT_ALL * 1024], BF16))
    xnT_own = ctx.enter_context(nc.sbuf_tensor("xnT_own", [128, NDC * T_OWN], BF16))
    xnT_halo = ctx.enter_context(nc.sbuf_tensor("xnT_halo", [128, NDC * T_HALO], BF16))

    xnT_own3 = xnT_own[:].rearrange("p (dc t) -> p dc t", dc=NDC)
    xnT_halo3 = xnT_halo[:].rearrange("p (dc t) -> p dc t", dc=NDC)
    v_sb3 = v_sb[:].rearrange("p (mt o) -> p mt o", mt=NT_ALL)

    with (
        nc.sbuf_tensor("x_sb", [128, 2, D], F32) as x_sb,
        nc.sbuf_tensor("xn_bf", [128, D], BF16) as xn_bf,
        nc.sbuf_tensor("wkv_bf", [128, 2, D], BF16) as wkv_bf,
        nc.sbuf_tensor("vT_tmp", [128, 1, T_ALL], BF16) as vT_tmp,
        nc.psum_tensor("ps_t", [128, 2, 512], F32) as ps_t,
        nc.psum_tensor("ps_kv", [128, 2, T_ALL], F32) as ps_kv,
        nc.Block() as b1,
    ):
        def xe_dst(i):
            if i < 4:
                return xnT_halo3, i * 128
            return xnT_own3, (i - 4) * 128

        @b1.sync
        def _(sy):
            sy.dma_start(iden[:], idend).then_inc(c_ldi.sem, 16)
            sy.dma_start(btr[:], btrd).then_inc(c_ldb.sem, 16)
            for i in range(NT_ALL):
                if i >= 2:
                    sy.wait_ge(c_axn.sem, i - 1)
                c_ldx.inc(sy.dma_start(x_sb[:, i % 2, :],
                                       xs[i * 128:(i + 1) * 128, :]), i)
        c_ldi.val += 16
        c_ldb.val += 16

        @b1.vector
        def _(v):
            c_vini.inc(v.memset(epsc[:], 1e-5))
            v.wait_ge(c_ldb.sem, 16)
            c_vini.inc(v.tensor_scalar_mul(nbtr[:], btr[:], -1.0))
            c_vini.inc(v.memset(ones[:], 0.0))
            v.wait_ge(c_vini.sem, 3)
            c_vini.inc(v.memset(ones[0:1, :], 1.0))
            for i in range(NT_ALL):
                p = i % 2
                c_ldx.wait(v, i)
                if i >= 2:
                    v.wait_ge(c_vnmi.sem, i - 1)
                for ch in range(8):
                    c_vbs.inc(v.bn_stats(bn6[:, p, ch, :],
                                         x_sb[:, p, ch * 512:(ch + 1) * 512]))
                v.wait_ge(c_vbs.sem, c_vbs.val)
                c_vbn.inc(v.bn_aggr(bn2[:, p, :], bn6[:, p, :, :]))
                if i >= 1 and sub >= 3:
                    dst3, off = xe_dst(i - 1)
                    for g in range(8):
                        gi = (i - 1) * 8 + g
                        v.wait_ge(c_txt.sem, gi + 1)
                        c_vxe.inc(v.tensor_copy(
                            dst3[:, 4 * g:4 * g + 4, off:off + 128],
                            ps_t[:, gi % 2, :].rearrange("p (c t) -> p c t", c=4)))
                v.wait_ge(c_ainv.sem, i + 1)
                c_vnmi.inc(v.tensor_scalar(cols[:, p, 2:3], bn2[:, p, 0:1],
                                           cols[:, p, 1:2], -1.0,
                                           ALU.mult, ALU.mult))
            if sub >= 3:
                dst3, off = xe_dst(NT_ALL - 1)
                for g in range(8):
                    gi = (NT_ALL - 1) * 8 + g
                    v.wait_ge(c_txt.sem, gi + 1)
                    c_vxe.inc(v.tensor_copy(
                        dst3[:, 4 * g:4 * g + 4, off:off + 128],
                        ps_t[:, gi % 2, :].rearrange("p (c t) -> p c t", c=4)))

        @b1.scalar
        def _(s):
            s.wait_ge(c_vini.sem, 2)
            for i in range(NT_ALL):
                p = i % 2
                s.wait_ge(c_vbn.sem, i + 1)
                c_aln.inc(s.activation(cols[:, p, 0:1], bn2[:, p, 1:2], AF.Ln,
                                       bias=epsc[:], scale=1.0))
                s.wait_ge(c_aln.sem, c_aln.val)
                c_ainv.inc(s.activation(cols[:, p, 1:2], cols[:, p, 0:1], AF.Exp,
                                        bias=0.0, scale=-0.5))
                s.wait_ge(c_vnmi.sem, i + 1)
                if i >= 1 and sub >= 2:
                    s.wait_ge(c_txt.sem, 8 * i)
                c_axn.inc(s.activation(xn_bf[:], x_sb[:, p, :], AF.Identity,
                                       bias=cols[:, p, 2:3], scale=cols[:, p, 1:2]))

        @b1.tensor
        def _(t):
            if sub < 2:
                return
            t.wait_ge(c_ldi.sem, 16)
            for i in range(NT_ALL):
                t.wait_ge(c_axn.sem, i + 1)
                for g in range(8):
                    gi = i * 8 + g
                    if gi >= 2 and sub >= 3:
                        t.wait_ge(c_vxe.sem, gi - 1)
                    for q in range(4):
                        dc = 4 * g + q
                        mi = t.matmul(ps_t[:, gi % 2, q * 128:(q + 1) * 128],
                                      xn_bf[:, dc * 128:(dc + 1) * 128], iden[:],
                                      start=True, stop=True)
                    c_txt.inc(mi)

        @b1.gpsimd
        def _(g):
            if sub < 4:
                return
            for j in range(16):
                if j >= 2 and sub >= 5:
                    g.wait_ge(c_tkv.sem, j - 1)
                src = wkt[j] if j < 8 else wvt[j - 8]
                c_gwkv.inc(g.dma_start(wkv_bf[:, j % 2, :],
                                       src.rearrange("p dc o -> p (dc o)")), j)

        def emit_vtr(t, jv):
            ocv = jv - 8
            t.wait_ge(c_akv.sem, jv + 1)
            for g in range(3):
                gi3 = ocv * 3 + g
                if gi3 >= 2:
                    t.wait_ge(c_vve.sem, gi3 - 1)
                for q in range(4):
                    mt = 4 * g + q
                    mi = t.matmul(ps_t[:, gi3 % 2, q * 128:(q + 1) * 128],
                                  vT_tmp[:, 0, mt * 128:(mt + 1) * 128], iden[:],
                                  start=True, stop=True)
                c_tvt.inc(mi)

        @b1.tensor
        def _(t):
            if sub < 5:
                return
            t.wait_ge(c_vxe.sem, 96)
            for j in range(16):
                c_gwkv.wait(t, j)
                if j >= 2:
                    t.wait_ge(c_akv.sem, j - 1)
                for dc in range(NDC):
                    lhs = wkv_bf[:, j % 2, dc * 128:(dc + 1) * 128]
                    t.matmul(ps_kv[:, j % 2, 0:512],
                             lhs, xnT_halo[:, dc * 512:(dc + 1) * 512],
                             start=(dc == 0), stop=(dc == NDC - 1))
                    t.matmul(ps_kv[:, j % 2, 512:1024],
                             lhs, xnT_own[:, dc * 1024:dc * 1024 + 512],
                             start=(dc == 0), stop=(dc == NDC - 1))
                    mi = t.matmul(ps_kv[:, j % 2, 1024:1536],
                                  lhs, xnT_own[:, dc * 1024 + 512:(dc + 1) * 1024],
                                  start=(dc == 0), stop=(dc == NDC - 1))
                c_tkv.inc(mi)
                if j >= 9 and sub >= 6:
                    emit_vtr(t, j - 1)
            if sub >= 6:
                emit_vtr(t, 15)

        @b1.scalar
        def _(s):
            if sub < 5:
                return
            for j in range(16):
                oc = j % 8
                s.wait_ge(c_tkv.sem, j + 1)
                if j < 8:
                    c_akv.inc(s.activation(kT[:, oc * T_ALL:(oc + 1) * T_ALL],
                                           ps_kv[:, j % 2, :], AF.Identity,
                                           bias=0.0, scale=1.0))
                else:
                    if j >= 9 and sub >= 6:
                        s.wait_ge(c_tvt.sem, 3 * (j - 8))
                    c_akv.inc(s.activation(vT_tmp[:, 0, :],
                                           ps_kv[:, j % 2, :], AF.Identity,
                                           bias=0.0, scale=1.0))

        @b1.vector
        def _(v):
            if sub < 6:
                return
            for gi3 in range(24):
                ocv, g = divmod(gi3, 3)
                v.wait_ge(c_tvt.sem, gi3 + 1)
                c_vve.inc(v.tensor_copy(
                    v_sb3[:, 4 * g:4 * g + 4, ocv * 128:(ocv + 1) * 128],
                    ps_t[:, gi3 % 2, :].rearrange("p (c t) -> p c t", c=4)))

    if phases <= 1:
        if dump and sub >= 6:
            d_kT = nc.dram_tensor("d_kT", [128, NKV * T_ALL], BF16, kind="ExternalOutput").ap()
            d_v = nc.dram_tensor("d_v", [128, NT_ALL * 1024], BF16, kind="ExternalOutput").ap()
            d_xo = nc.dram_tensor("d_xo", [128, NDC * T_OWN], BF16, kind="ExternalOutput").ap()
            d_xh = nc.dram_tensor("d_xh", [128, NDC * T_HALO], BF16, kind="ExternalOutput").ap()
            c_dd = Ctr(nc, "dd")
            with nc.Block() as bd:
                @bd.sync
                def _(sy):
                    sy.wait_ge(c_vve.sem, 24)
                    sy.wait_ge(c_akv.sem, 16)
                    c_dd.inc(sy.dma_start(d_kT, kT[:]), 16)
                    c_dd.inc(sy.dma_start(d_v, v_sb[:]), 16)
                    c_dd.inc(sy.dma_start(d_xo, xnT_own[:]), 16)
                    c_dd.inc(sy.dma_start(d_xh, xnT_halo[:]), 16)
                    sy.wait_ge(c_dd.sem, 64)
        ctx.close()
        nc.finalize()
        return nc

    with (
        nc.sbuf_tensor("msk", [128, NT_OWN * WIN * 128], BF16) as msk,
        nc.sbuf_tensor("wq_bf", [128, 2, D], BF16) as wq_bf,
        nc.sbuf_tensor("qT", [128, 2, T_OWN], BF16) as qT,
        nc.sbuf_tensor("pt", [128, 2, WIN * 128], BF16) as pt,
        nc.sbuf_tensor("ptT", [128, 2, WIN * 128], BF16) as ptT,
        nc.sbuf_tensor("o_sb", [128, 2, NT_OWN * HD], BF16) as o_sb,
        nc.sbuf_tensor("scols", [128, 2, 4], F32) as scols,
        nc.psum_tensor("ps_sa", [128, WIN * 128], F32) as ps_sa,
        nc.psum_tensor("ps_sb", [128, WIN * 128], F32) as ps_sb,
        nc.psum_tensor("ps_pt", [128, WIN * 128], F32) as ps_pt,
        nc.psum_tensor("ps_q", [128, 512], F32) as ps_q,
        nc.psum_tensor("ps_o", [128, 128], F32) as ps_o,
        nc.Block() as b2,
    ):
        @b2.sync
        def _(sy):
            sy.wait_ge(c_vve.sem, 24)
            c_ldm.inc(sy.dma_start(msk[:], mskd.rearrange("p a b c -> p (a b c)")), 16)
            for h in range(NQH):
                sy.wait_ge(c_aoe.sem, NT_OWN * (h + 1))
                c_aod.inc(sy.dma_start(o_s[h].rearrange("qt p dh -> p qt dh"),
                                       o_sb[:, h % 2, :].rearrange(
                                           "p (qt dh) -> p qt dh", qt=NT_OWN)), h)

        @b2.gpsimd
        def _(g):
            g.wait_ge(c_vve.sem, 24)
            for h in range(NQH):
                if h >= 2:
                    g.wait_ge(c_tq.sem, 2 * (h - 1))
                c_gwq.inc(g.dma_start(wq_bf[:, h % 2, :],
                                      wqt[h].rearrange("p dc o -> p (dc o)")), h)

        def pe_s(t, u):
            h, qt = divmod(u, NT_OWN)
            kv = h // 4
            t.wait_ge(c_aqe.sem, 2 * (h + 1))
            if u >= 2:
                t.wait_ge(c_aexp.sem, u - 1)
            if u == 0:
                t.wait_ge(c_ldm.sem, 16)
            pss = ps_sa if u % 2 == 0 else ps_sb
            m0 = kv * T_ALL + qt * 128
            mb = qt * WIN * 128
            lhs = qT[:, h % 2, qt * 128:(qt + 1) * 128]
            t.matmul(pss[:, 0:512], lhs, kT[:, m0:m0 + 512],
                     start=True, stop=False)
            t.matmul(pss[:, 0:512], iden[:], msk[:, mb:mb + 512],
                     start=False, stop=True)
            t.matmul(pss[:, 512:640], lhs, kT[:, m0 + 512:m0 + 640],
                     start=True, stop=False)
            c_ts.inc(t.matmul(pss[:, 512:640], iden[:], msk[:, mb + 512:mb + 640],
                              start=False, stop=True))

        def pe_pt(t, u):
            t.wait_ge(c_aexp.sem, u + 1)
            t.wait_ge(c_vpte.sem, u)
            for j in range(WIN):
                mi = t.matmul(ps_pt[:, j * 128:(j + 1) * 128],
                              pt[:, u % 2, j * 128:(j + 1) * 128], iden[:],
                              start=True, stop=True)
            c_tpt.inc(mi)

        def pe_o(t, u):
            h, qt = divmod(u, NT_OWN)
            kv = h // 4
            t.wait_ge(c_vpte.sem, u + 1)
            if u >= 1:
                t.wait_ge(c_aoe.sem, u)
            pso = ps_o
            for j in range(WIN):
                mt = qt + j
                mi = t.matmul(pso[:, :],
                              ptT[:, u % 2, j * 128:(j + 1) * 128],
                              v_sb[:, mt * 1024 + kv * 128:mt * 1024 + (kv + 1) * 128],
                              start=(j == 0), stop=(j == WIN - 1))
            c_to.inc(mi)

        def pe_q_slice(t, h, qt):
            tc, sub4 = divmod(qt, 4)
            if qt == 0:
                c_gwq.wait(t, h)
            if sub4 == 0:
                t.wait_ge(c_aqe.sem, 2 * h + tc)
            for i in range(8):
                dc = sub4 * 8 + i
                mi = t.matmul(ps_q[:, :],
                              wq_bf[:, h % 2, dc * 128:(dc + 1) * 128],
                              xnT_own[:, dc * 1024 + tc * 512:dc * 1024 + (tc + 1) * 512],
                              start=(sub4 == 0 and i == 0),
                              stop=(sub4 == 3 and i == 7))
            if sub4 == 3:
                c_tq.inc(mi)

        @b2.tensor
        def _(t):
            for qt in range(NT_OWN):
                pe_q_slice(t, 0, qt)
            for h in range(NQH):
                for qt in range(NT_OWN):
                    u = h * NT_OWN + qt
                    if u >= 2:
                        pe_o(t, u - 2)
                    if u >= 1:
                        pe_pt(t, u - 1)
                    pe_s(t, u)
                    if h + 1 < NQH:
                        pe_q_slice(t, h + 1, qt)
            pe_pt(t, NU - 1)
            pe_o(t, NU - 2)
            pe_o(t, NU - 1)

        @b2.vector
        def _(v):
            for u in range(NU):
                if u >= 1:
                    v.wait_ge(c_tpt.sem, u)
                    if u >= 3:
                        v.wait_ge(c_to.sem, u - 2)
                    c_vpte.inc(v.tensor_copy(ptT[:, (u - 1) % 2, :], ps_pt[:]))
                    if u >= 3:
                        v.wait_ge(c_aoe.sem, u - 2)
                    c_vc.inc(v.reciprocal(scols[:, (u - 1) % 2, 2:3],
                                          scols[:, (u - 1) % 2, 1:2]))
                v.wait_ge(c_ts.sem, u + 1)
                if u >= 2:
                    v.wait_ge(c_aexp.sem, u - 1)
                c_vnm.inc(v.reduce_max(scols[:, u % 2, 0:1],
                                       (ps_sa if u % 2 == 0 else ps_sb)[:, 0:640],
                                       axis=AX.X, negate=True))
            v.wait_ge(c_tpt.sem, NU)
            c_vpte.inc(v.tensor_copy(ptT[:, (NU - 1) % 2, :], ps_pt[:]))
            v.wait_ge(c_aoe.sem, NU - 2)
            c_vc.inc(v.reciprocal(scols[:, (NU - 1) % 2, 2:3],
                                  scols[:, (NU - 1) % 2, 1:2]))

        def oevac(s, u):
            h, qt = divmod(u, NT_OWN)
            s.wait_ge(c_to.sem, u + 1)
            s.wait_ge(c_vc.sem, u + 1)
            if qt == 0 and h >= 2:
                c_aod.wait(s, h - 2)
            c_aoe.inc(s.activation(o_sb[:, h % 2, qt * 128:(qt + 1) * 128],
                                   ps_o[:], AF.Identity, bias=0.0,
                                   scale=scols[:, u % 2, 2:3]))


        def qevac(s, h, tc):
            s.wait_ge(c_tq.sem, 2 * h + tc + 1)
            if h >= 2 and tc == 0:
                s.wait_ge(c_ts.sem, NT_OWN * (h - 1))
            c_aqe.inc(s.activation(qT[:, h % 2, tc * 512:(tc + 1) * 512],
                                   ps_q[:], AF.Identity,
                                   bias=0.0, scale=QSCALE))

        @b2.scalar
        def _(s):
            qevac(s, 0, 0)
            qevac(s, 0, 1)
            for h in range(NQH):
                for qt in range(NT_OWN):
                    u = h * NT_OWN + qt
                    if u >= 2:
                        oevac(s, u - 2)
                    s.wait_ge(c_vnm.sem, u + 1)
                    if u >= 2:
                        s.wait_ge(c_tpt.sem, u - 1)
                        s.wait_ge(c_vc.sem, u - 1)
                    c_aexp.inc(s.activation(pt[:, u % 2, :],
                                            (ps_sa if u % 2 == 0 else ps_sb)[:, 0:640],
                                            AF.Exp,
                                            bias=scols[:, u % 2, 0:1], scale=1.0,
                                            accum_out=scols[:, u % 2, 1:2]))
                    if h + 1 < NQH and qt in (3, 7):
                        qevac(s, h + 1, 0 if qt == 3 else 1)
            oevac(s, NU - 2)
            oevac(s, NU - 1)

    ctx.close()

    if phases <= 2:
        nc.finalize()
        return nc

    with (
        nc.sbuf_tensor("o_row", [128, 2, D], BF16) as o_row,
        nc.sbuf_tensor("oT", [128, 32 * T_OWN], BF16) as oT,
        nc.sbuf_tensor("wo_bf", [128, 2, 32 * 512], BF16) as wo_bf,
        nc.sbuf_tensor("stg3", [128, 2, 512], F32) as stg3,
    ):
        oT3 = oT[:].rearrange("p (oc t) -> p oc t", oc=32)
        with (
            nc.psum_tensor("ps_t3a", [128, 2, 512], F32) as ps_t3a,
            nc.Block() as b3a,
        ):
            @b3a.sync
            def _(sy):
                c_aod.wait_all(sy, NQH)
                for tt in range(NT_OWN):
                    if tt >= 2:
                        sy.wait_ge(c_tot.sem, 8 * (tt - 1))
                    c_ldoa.inc(sy.dma_start(
                        o_row[:, tt % 2, 0:2048].rearrange("p (h dh) -> p h dh", h=16),
                        o_s[0:16, tt, :, :].rearrange("h p dh -> p h dh")), tt)
                    c_ldob.inc(sy.dma_start(
                        o_row[:, tt % 2, 2048:4096].rearrange("p (h dh) -> p h dh", h=16),
                        o_s[16:32, tt, :, :].rearrange("h p dh -> p h dh")), tt)

            @b3a.tensor
            def _(t):
                for tt in range(NT_OWN):
                    c_ldoa.wait(t, tt)
                    c_ldob.wait(t, tt)
                    for g in range(8):
                        gi = tt * 8 + g
                        if gi >= 2:
                            t.wait_ge(c_voe.sem, gi - 1)
                        for q in range(4):
                            oc = 4 * g + q
                            mi = t.matmul(ps_t3a[:, gi % 2, q * 128:(q + 1) * 128],
                                          o_row[:, tt % 2, oc * 128:(oc + 1) * 128],
                                          iden[:], start=True, stop=True)
                        c_tot.inc(mi)

            @b3a.vector
            def _(v):
                for gi in range(64):
                    tt, g = divmod(gi, 8)
                    v.wait_ge(c_tot.sem, gi + 1)
                    c_voe.inc(v.tensor_copy(
                        oT3[:, 4 * g:4 * g + 4, tt * 128:(tt + 1) * 128],
                        ps_t3a[:, gi % 2, :].rearrange("p (c t) -> p c t", c=4)))

        with (
            nc.psum_tensor("ps_wo", [128, NT_OWN, 512], F32) as ps_wo,
            nc.Block() as b3w,
        ):
            @b3w.gpsimd
            def _(g):
                c_aod.wait_all(g, NQH)
                for db in range(8):
                    if db >= 2:
                        g.wait_ge(c_twog.sem, 8 * (db - 1))
                    c_gwo.inc(g.dma_start(
                        wo_bf[:, db % 2, :].rearrange("p (oc dd) -> p oc dd", oc=32),
                        wot[db].rearrange("oc p dd -> p oc dd")), db)

            @b3w.tensor
            def _(t):
                t.wait_ge(c_voe.sem, 64)
                for db in range(8):
                    c_gwo.wait(t, db)
                    for tt in range(NT_OWN):
                        if db >= 1:
                            t.wait_ge(c_awoc.sem, 8 * (db - 1) + tt + 1)
                        for oc in range(32):
                            mi = t.matmul(ps_wo[:, tt, :],
                                          oT3[:, oc, tt * 128:(tt + 1) * 128],
                                          wo_bf[:, db % 2, oc * 512:(oc + 1) * 512],
                                          start=(oc == 0), stop=(oc == 31))
                        c_twog.inc(mi)

            @b3w.scalar
            def _(s):
                for k in range(64):
                    db, tt = divmod(k, 8)
                    s.wait_ge(c_twog.sem, k + 1)
                    if k >= 2:
                        s.wait_ge(c_vwos.sem, k - 1)
                        c_awod.wait(s, k - 2)
                    c_awoc.inc(s.activation(stg3[:, k % 2, :], ps_wo[:, tt, :],
                                            AF.Identity, bias=0.0, scale=1.0))

            @b3w.sync
            def _(sy):
                for k in range(64):
                    db, tt = divmod(k, 8)
                    sy.wait_ge(c_awoc.sem, k + 1)
                    c_awod.inc(sy.dma_start(
                        attn_s[tt * 128:(tt + 1) * 128, db * 512:(db + 1) * 512],
                        stg3[:, k % 2, :]), k)

            @b3w.vector
            def _(v):
                for k in range(64):
                    db, tt = divmod(k, 8)
                    v.wait_ge(c_awoc.sem, k + 1)
                    c_vwos.inc(v.bn_stats(bn6a[:, tt, db, :], stg3[:, k % 2, :]))

    if phases <= 3:
        nc.finalize()
        return nc

    s5a = ExitStack()
    x2nT = s5a.enter_context(nc.sbuf_tensor("x2nT", [128, NDC * T_OWN], BF16))
    hpT = s5a.enter_context(nc.sbuf_tensor("hpT", [128, NT_OWN * 128], BF16))
    hcat = s5a.enter_context(nc.sbuf_tensor("hcat", [128, NDC * 137], BF16))
    x2nT3 = x2nT[:].rearrange("p (dc t) -> p dc t", dc=NDC)

    P3W_DONE = 16 * 64

    with (
        nc.sbuf_tensor("a_row", [128, 2, D], F32) as a_row,
        nc.sbuf_tensor("x_row", [128, 2, D], F32) as x_row,
        nc.sbuf_tensor("lt_row", [128, D], F32) as lt_row,
        nc.sbuf_tensor("x1_row", [128, 2, D], F32) as x1_row,
        nc.sbuf_tensor("x2n_bf", [128, D], BF16) as x2n_bf,
        nc.sbuf_tensor("bn6b", [128, 2, 8, 6], F32) as bn6b,
        nc.sbuf_tensor("bn2b", [128, 2, 4], F32) as bn2b,
        nc.sbuf_tensor("cols3", [128, 2, 8], F32) as cols3,
        nc.sbuf_tensor("g8", [128, 8], F32) as g8,
        nc.sbuf_tensor("gcols", [128, 8], F32) as gcols,
        nc.sbuf_tensor("hp_bf", [128, 128], BF16) as hp_bf,
        nc.psum_tensor("ps_t3b", [128, 2, 512], F32) as ps_t3b,
        nc.psum_tensor("ps_h", [128, 4, 512], F32) as ps_h,
        nc.Block() as b4,
    ):
        def hslice(tt):
            bank, reg = divmod(tt, 2)
            return ps_h[:, bank, reg * 256:reg * 256 + 137]

        @b4.sync
        def _(sy):
            c_awod.wait_all(sy, 64)
            for tt in range(NT_OWN):
                if tt >= 2:
                    sy.wait_ge(c_anorm.sem, tt - 1)
                c_lda.inc(sy.dma_start(a_row[:, tt % 2, :],
                                       attn_s[tt * 128:(tt + 1) * 128, :]), tt)
                if tt >= 2:
                    sy.wait_ge(c_vadd.sem, tt - 1)
                c_ldxr.inc(sy.dma_start(
                    x_row[:, tt % 2, :],
                    xs[T_HALO + tt * 128:T_HALO + (tt + 1) * 128, :]), tt)
                sy.wait_ge(c_vadd.sem, tt + 1)
                c_ax1d.inc(sy.dma_start(x1_s[tt * 128:(tt + 1) * 128, :],
                                        x1_row[:, tt % 2, :]), tt)

        @b4.gpsimd
        def _(g):
            g.wait_ge(c_twog.sem, 64)
            c_awod.wait_all(g, 64)
            c_ldgh.inc(g.dma_start(
                hcat[:].rearrange("p (dc j) -> p dc j", dc=32),
                hct.rearrange("dc p j -> p dc j")), 16)
            for tt in range(NT_OWN):
                p = tt % 2
                g.wait_ge(c_anorm.sem, tt + 1)
                c_ldxr.wait(g, tt)
                if tt >= 2:
                    g.wait_ge(c_vx2n.sem, tt - 1)
                    c_ax1d.wait(g, tt - 2)
                c_vadd.inc(g.tensor_add(x1_row[:, p, :], x_row[:, p, :], lt_row[:]))

        def v_gating(v, tt):
            hs = hslice(tt)

            def step(inst):
                c_vgc.inc(inst)
                v.wait_ge(c_vgc.sem, c_vgc.val)

            v.wait_ge(c_thg.sem, tt + 2 if tt % 2 == 0 else tt + 1)
            c_vg1.inc(v.reduce_max(gcols[:, 0:1], hs[:, 128:136],
                                   axis=AX.X, negate=True))
            v.wait_ge(c_ag2.sem, 2 * (tt + 1))
            step(v.reciprocal(gcols[:, 2:3], gcols[:, 1:2]))
            step(v.tensor_scalar_add(gcols[:, 4:5], gcols[:, 3:4], 1.0))
            step(v.reciprocal(gcols[:, 5:6], gcols[:, 4:5]))
            step(v.tensor_scalar_mul(gcols[:, 6:7], gcols[:, 5:6], -0.125))
            step(v.tensor_scalar(g8[:], g8[:], gcols[:, 2:3], gcols[:, 6:7],
                                 ALU.mult, ALU.add))
            step(v.tensor_scalar_max(g8[:], g8[:], 0.0))
            step(v.reduce_sum(gcols[:, 7:8], g8[:], axis=AX.X))
            step(v.tensor_single_scalar(gcols[:, 0:1], gcols[:, 7:8], 0.0,
                                        ALU.is_equal))
            step(v.tensor_add(gcols[:, 1:2], gcols[:, 7:8], gcols[:, 0:1]))
            step(v.reciprocal(gcols[:, 4:5], gcols[:, 1:2]))
            step(v.tensor_scalar_mul(gcols[:, 5:6], gcols[:, 4:5], 2.0))
            step(v.tensor_scalar(g8[:], g8[:], gcols[:, 5:6], None, ALU.mult))
            if tt >= 1:
                v.wait_ge(c_tht.sem, tt)
            for e in range(8):
                mi = v.tensor_scalar(hp_bf[:, e * 16:(e + 1) * 16],
                                     hs[:, e * 16:(e + 1) * 16],
                                     g8[:, e:e + 1], None, ALU.mult)
                if e < 7:
                    c_vgc.inc(mi)
                else:
                    c_vg3.inc(mi)
            v.wait_ge(c_vgc.sem, c_vgc.val)
            v.wait_ge(c_tht.sem, tt + 1)
            c_vhte.inc(v.tensor_copy(hpT[:, tt * 128:(tt + 1) * 128],
                                     ps_t3b[:, tt % 2, 0:128]))

        @b4.vector
        def _(v):
            v.wait_ge(c_twog.sem, 64)
            c_awod.wait_all(v, 64)
            for tt in range(NT_OWN):
                p = tt % 2
                v.wait_ge(c_vwos.sem, 56 + tt + 1)
                if tt >= 2:
                    v.wait_ge(c_anorm.sem, tt - 1)
                c_vagg.inc(v.bn_aggr(cols3[:, p, 4:6], bn6a[:, tt, :, :]))
                v.wait_ge(c_ainv3.sem, tt + 1)
                c_vnmi3.inc(v.tensor_scalar(cols3[:, p, 2:3], cols3[:, p, 4:5],
                                            cols3[:, p, 1:2], -1.0,
                                            ALU.mult, ALU.mult))
                v.wait_ge(c_vadd.sem, tt + 1)
                for ch in range(8):
                    c_vbs2.inc(v.bn_stats(bn6b[:, p, ch, :],
                                          x1_row[:, p, ch * 512:(ch + 1) * 512]))
                v.wait_ge(c_vbs2.sem, c_vbs2.val)
                c_vbnb.inc(v.bn_aggr(bn2b[:, p, 0:2], bn6b[:, p, :, :]))
                v.wait_ge(c_ainv2.sem, tt + 1)
                if tt >= 1:
                    v.wait_ge(c_tx2t.sem, 8 * tt)
                c_vx2n.inc(v.tensor_scalar(x2n_bf[:], x1_row[:, p, :],
                                           bn2b[:, p, 0:1], bn2b[:, p, 3:4],
                                           ALU.subtract, ALU.mult))
                for g in range(8):
                    gi = tt * 8 + g
                    v.wait_ge(c_tx2t.sem, gi + 1)
                    c_vx2e.inc(v.tensor_copy(
                        x2nT3[:, 4 * g:4 * g + 4, tt * 128:(tt + 1) * 128],
                        ps_t3b[:, gi % 2, :].rearrange("p (c t) -> p c t", c=4)))
                if tt >= 1:
                    v_gating(v, tt - 1)
            v_gating(v, NT_OWN - 1)

        @b4.scalar
        def _(s):
            c_awod.wait_all(s, 64)
            s.wait_ge(c_vini.sem, 2)
            for tt in range(NT_OWN):
                p = tt % 2
                s.wait_ge(c_vagg.sem, tt + 1)
                c_aln3.inc(s.activation(cols3[:, p, 0:1], cols3[:, p, 5:6], AF.Ln,
                                        bias=epsc[:], scale=1.0))
                s.wait_ge(c_aln3.sem, c_aln3.val)
                c_ainv3.inc(s.activation(cols3[:, p, 1:2], cols3[:, p, 0:1],
                                         AF.Exp, bias=0.0, scale=-0.5))
                s.wait_ge(c_vnmi3.sem, tt + 1)
                c_lda.wait(s, tt)
                if tt >= 1:
                    s.wait_ge(c_vadd.sem, tt)
                c_anorm.inc(s.activation(lt_row[:], a_row[:, p, :], AF.Identity,
                                         bias=cols3[:, p, 2:3],
                                         scale=cols3[:, p, 1:2]))

                s.wait_ge(c_vbnb.sem, tt + 1)
                c_aln2b.inc(s.activation(bn2b[:, p, 2:3], bn2b[:, p, 1:2], AF.Ln,
                                         bias=epsc[:], scale=1.0))
                s.wait_ge(c_aln2b.sem, c_aln2b.val)
                c_ainv2.inc(s.activation(bn2b[:, p, 3:4], bn2b[:, p, 2:3],
                                         AF.Exp, bias=0.0, scale=-0.5))
                if tt >= 1:
                    ttp = tt - 1
                    hs = hslice(ttp)
                    s.wait_ge(c_vg1.sem, ttp + 1)
                    s.wait_ge(c_vg3.sem, ttp)
                    s.activation(g8[:], hs[:, 128:136], AF.Exp,
                                 bias=gcols[:, 0:1], scale=1.0,
                                 accum_out=gcols[:, 1:2])
                    c_ag2.inc(s.activation(gcols[:, 3:4], hs[:, 136:137], AF.Exp,
                                           bias=nbtr[:], scale=-1.0), 2)
            ttp = NT_OWN - 1
            hs = hslice(ttp)
            s.wait_ge(c_vg1.sem, ttp + 1)
            s.wait_ge(c_vg3.sem, ttp)
            s.activation(g8[:], hs[:, 128:136], AF.Exp,
                         bias=gcols[:, 0:1], scale=1.0,
                         accum_out=gcols[:, 1:2])
            c_ag2.inc(s.activation(gcols[:, 3:4], hs[:, 136:137], AF.Exp,
                                   bias=nbtr[:], scale=-1.0), 2)

        def emit_hptr(t, tt):
            t.wait_ge(c_vg3.sem, tt + 1)
            if tt >= 2:
                t.wait_ge(c_vhte.sem, tt - 1)
            c_tht.inc(t.matmul(ps_t3b[:, tt % 2, 0:128], hp_bf[:], iden[:],
                               start=True, stop=True))

        @b4.tensor
        def _(t):
            t.wait_ge(c_ldi.sem, 16)
            t.wait_ge(c_ldgh.sem, 16)
            for tt in range(NT_OWN):
                t.wait_ge(c_vx2n.sem, tt + 1)
                for g in range(8):
                    gi = tt * 8 + g
                    if gi >= 2:
                        t.wait_ge(c_vx2e.sem, gi - 1)
                    for q in range(4):
                        dc = 4 * g + q
                        mi = t.matmul(ps_t3b[:, gi % 2, q * 128:(q + 1) * 128],
                                      x2n_bf[:, dc * 128:(dc + 1) * 128], iden[:],
                                      start=True, stop=True)
                    c_tx2t.inc(mi)
                t.wait_ge(c_vx2e.sem, 8 * (tt + 1))
                for dc in range(NDC):
                    mi = t.matmul(hslice(tt),
                                  x2nT3[:, dc, tt * 128:(tt + 1) * 128],
                                  hcat[:, dc * 137:(dc + 1) * 137],
                                  start=(dc == 0), stop=(dc == NDC - 1))
                c_thg.inc(mi)
                if tt >= 1:
                    emit_hptr(t, tt - 1)
            emit_hptr(t, NT_OWN - 1)

    if phases <= 4:
        if 300 <= sub < 400:
            extra = [nc.alloc_semaphore(f"pad{i}") for i in range(sub - 300)]
            with nc.Block() as bps:
                @bps.sync
                def _(sy):
                    for e in extra:
                        sy.nop().then_inc(e, 1)
                        sy.wait_ge(e, 1)
        if sub in (62, 63, 64):
            with nc.Block() as bpad:
                if sub == 62:
                    @bpad.vector
                    def _(v):
                        for _ in range(400):
                            v.engine_nop()
                elif sub == 63:
                    @bpad.tensor
                    def _(t):
                        for _ in range(2200):
                            t.nop()
                elif sub >= 64:
                    @bpad.scalar
                    def _(srr):
                        for _ in range(sub - 64):
                            srr.engine_nop()
        s5a.close()
        nc.finalize()
        return nc

    snap4 = snapshot()
    with (
        nc.sbuf_tensor("wb_bf", [128, 2, 32 * 512], BF16) as wb_bf,
        nc.sbuf_tensor("bT_bf", [128, 4096], BF16) as bT_bf,
        nc.sbuf_tensor("bb", [128, 4096], BF16) as bb,
        nc.sbuf_tensor("stg4", [128, 2, 512], F32) as stg4,
        nc.psum_tensor("ps_f", [128, NT_OWN, 512], F32) as ps_f,
        nc.Block() as b5,
    ):
        @b5.gpsimd
        def _(g):
            g.wait_ge(c_bar4.sem, 1)
            if sub == 70:
                return
            c_ldgt.inc(g.dma_start(bT_bf[:], bTd), 16)
            c_ldgt.inc(g.dma_start(bb[:], bbd), 16)
            for ob in range(8):
                if ob >= 2 and sub >= 72:
                    g.wait_ge(c_tfg.sem, 8 * (ob - 1))
                c_gwb.inc(g.dma_start(
                    wb_bf[:, ob % 2, :].rearrange("p (dc oo) -> p dc oo", dc=32),
                    wbt[ob].rearrange("dc p oo -> p dc oo")), ob)
                if sub == 71:
                    c_gwb.wait(g, ob)

        @b5.sync
        def _(sy):
            barrier(sy, snap4)
            c_bar4.inc(sy.nop())

        @b5.tensor
        def _(t):
            if sub in (70, 71):
                return
            t.wait_ge(c_bar4.sem, 1)
            t.wait_ge(c_ldgt.sem, 32)
            t.wait_ge(c_ldgh.sem, 16)
            t.wait_ge(c_vini.sem, 4)
            for ob in range(8):
                c_gwb.wait(t, ob)
                for tt in range(NT_OWN):
                    if ob >= 1:
                        t.wait_ge(c_vfev.sem, 8 * (ob - 1) + tt + 1)
                    for dc in range(NDC):
                        t.matmul(ps_f[:, tt, :],
                                 x2nT3[:, dc, tt * 128:(tt + 1) * 128],
                                 wb_bf[:, ob % 2, dc * 512:(dc + 1) * 512],
                                 start=(dc == 0), stop=False)
                    t.matmul(ps_f[:, tt, :],
                             hpT[:, tt * 128:(tt + 1) * 128],
                             bT_bf[:, ob * 512:(ob + 1) * 512],
                             start=False, stop=False)
                    mi = t.matmul(ps_f[:, tt, :], ones[0:1, :],
                                  bb[0:1, ob * 512:(ob + 1) * 512],
                                  start=False, stop=True)
                    c_tfg.inc(mi)

        @b5.scalar
        def _(s):
            if sub in (70, 71, 72):
                return
            s.wait_ge(c_bar4.sem, 1)
            for k in range(64):
                ob, tt = divmod(k, 8)
                s.wait_ge(c_tfg.sem, k + 1)
                if k >= 2:
                    c_afd.wait(s, k - 2)
                    s.wait_ge(c_vfst.sem, k - 1)
                c_vfev.inc(s.activation(stg4[:, k % 2, :], ps_f[:, tt, :],
                                        AF.Identity, bias=0.0, scale=1.0))

        @b5.vector
        def _(v):
            if sub in (70, 71, 72, 79):
                return
            v.wait_ge(c_bar4.sem, 1)
            for k in range(64):
                ob, tt = divmod(k, 8)
                v.wait_ge(c_vfev.sem, k + 1)
                c_vfst.inc(v.bn_stats(bn6f[:, tt, ob, :], stg4[:, k % 2, :]))

        @b5.sync
        def _(sy2):
            if sub in (70, 71, 72, 73, 78, 79):
                return
            for k in range(64):
                ob, tt = divmod(k, 8)
                sy2.wait_ge(c_vfev.sem, k + 1)
                c_afd.inc(sy2.dma_start(
                    ffn_s[tt * 128:(tt + 1) * 128, ob * 512:(ob + 1) * 512],
                    stg4[:, k % 2, :]), k)

    s5a.close()

    P4B_DONE = 16 * 64

    if phases <= 5:
        nc.finalize()
        return nc

    snap5 = snapshot()
    with (
        nc.sbuf_tensor("f_row", [128, 2, D], F32) as f_row,
        nc.sbuf_tensor("x1r", [128, 2, D], F32) as x1r,
        nc.sbuf_tensor("lt5", [128, D], F32) as lt5,
        nc.sbuf_tensor("y_row", [128, 2, D], F32) as y_row,
        nc.sbuf_tensor("cols5", [128, 2, 8], F32) as cols5,
        nc.Block() as b6,
    ):
        @b6.sync
        def _(sy):
            barrier(sy, snap5)
            c_bar5.inc(sy.nop())
            for tt in range(NT_OWN):
                if tt >= 2:
                    sy.wait_ge(c_anorm5.sem, tt - 1)
                c_ldf.inc(sy.dma_start(f_row[:, tt % 2, :],
                                       ffn_s[tt * 128:(tt + 1) * 128, :]), tt)
                if tt >= 2:
                    sy.wait_ge(c_vy.sem, tt - 1)
                c_ldx1.inc(sy.dma_start(x1r[:, tt % 2, :],
                                        x1_s[tt * 128:(tt + 1) * 128, :]), tt)
                if tt >= 2:
                    sy.wait_ge(c_vy.sem, tt - 1)
                    c_ayd.inc(sy.dma_start(y[(tt - 2) * 128:(tt - 1) * 128, :],
                                           y_row[:, tt % 2, :]), tt - 2)
            for tt in (NT_OWN - 2, NT_OWN - 1):
                sy.wait_ge(c_vy.sem, tt + 1)
                c_ayd.inc(sy.dma_start(y[tt * 128:(tt + 1) * 128, :],
                                       y_row[:, tt % 2, :]), tt)
            c_ayd.wait_all(sy, NT_OWN)

        @b6.gpsimd
        def _(g):
            g.wait_ge(c_bar5.sem, 1)
            for tt in range(NT_OWN):
                p = tt % 2
                g.wait_ge(c_anorm5.sem, tt + 1)
                c_ldx1.wait(g, tt)
                if tt >= 2:
                    c_ayd.wait(g, tt - 2)
                c_vy.inc(g.tensor_add(y_row[:, p, :], x1r[:, p, :], lt5[:]))

        @b6.vector
        def _(v):
            v.wait_ge(c_bar5.sem, 1)
            for tt in range(NT_OWN):
                p = tt % 2
                if tt >= 2:
                    v.wait_ge(c_anorm5.sem, tt - 1)
                c_vagf.inc(v.bn_aggr(cols5[:, p, 4:6], bn6f[:, tt, :, :]))
                v.wait_ge(c_ainv5.sem, tt + 1)
                c_vnmi5.inc(v.tensor_scalar(cols5[:, p, 2:3], cols5[:, p, 4:5],
                                            cols5[:, p, 1:2], -1.0,
                                            ALU.mult, ALU.mult))


        @b6.scalar
        def _(s):
            s.wait_ge(c_bar5.sem, 1)
            for tt in range(NT_OWN):
                p = tt % 2
                s.wait_ge(c_vagf.sem, tt + 1)
                c_aln5.inc(s.activation(cols5[:, p, 0:1], cols5[:, p, 5:6], AF.Ln,
                                        bias=epsc[:], scale=1.0))
                s.wait_ge(c_aln5.sem, c_aln5.val)
                c_ainv5.inc(s.activation(cols5[:, p, 1:2], cols5[:, p, 0:1],
                                         AF.Exp, bias=0.0, scale=-0.5))
                s.wait_ge(c_vnmi5.sem, tt + 1)
                c_ldf.wait(s, tt)
                if tt >= 1:
                    s.wait_ge(c_vy.sem, tt)
                c_anorm5.inc(s.activation(lt5[:], f_row[:, p, :], AF.Identity,
                                          bias=cols5[:, p, 2:3],
                                          scale=cols5[:, p, 1:2]))


    nc.finalize()
    return nc


def _host_layouts(L0_attn_x, L0_attn_wq, L0_attn_wk, L0_attn_wv, L0_attn_wo,
                  L0_ffn_W_base, L0_ffn_b_base, L0_ffn_W_gate, L0_ffn_W_thr,
                  L0_ffn_b_thr, L0_ffn_A_w, L0_ffn_B_w):
    f32 = np.float32
    x2d = np.ascontiguousarray(np.asarray(L0_attn_x, dtype=f32).reshape(8192, D))
    wq = np.asarray(L0_attn_wq, dtype=f32)
    wk = np.asarray(L0_attn_wk, dtype=f32)
    wv = np.asarray(L0_attn_wv, dtype=f32)
    wo = np.asarray(L0_attn_wo, dtype=f32)
    wb = np.asarray(L0_ffn_W_base, dtype=f32)
    b_base = np.asarray(L0_ffn_b_base, dtype=f32)
    wg = np.asarray(L0_ffn_W_gate, dtype=f32)
    wt = np.asarray(L0_ffn_W_thr, dtype=f32)
    b_thr = np.asarray(L0_ffn_b_thr, dtype=f32)
    A_w = np.asarray(L0_ffn_A_w, dtype=f32)
    B_w = np.asarray(L0_ffn_B_w, dtype=f32)

    wqt = np.ascontiguousarray(wq.T.reshape(32, 128, 32, 128).transpose(2, 1, 0, 3))
    wkt = np.ascontiguousarray(wk.T.reshape(32, 128, 8, 128).transpose(2, 1, 0, 3))
    wvt = np.ascontiguousarray(wv.T.reshape(32, 128, 8, 128).transpose(2, 1, 0, 3))
    wot = np.ascontiguousarray(wo.T.reshape(32, 128, 8, 512).transpose(2, 0, 1, 3))
    wbt = np.ascontiguousarray(wb.T.reshape(32, 128, 8, 512).transpose(2, 0, 1, 3))
    G = np.concatenate([A_w.reshape(128, D), wg, wt[None, :]], axis=0)
    hct = np.ascontiguousarray(G.T.reshape(32, 128, 137))
    bTd = np.ascontiguousarray(B_w.transpose(0, 2, 1).reshape(128, D))
    bbd = np.ascontiguousarray(np.broadcast_to(b_base[None, :], (128, D)))
    btrd = np.ascontiguousarray(np.broadcast_to(b_thr.reshape(1, 1), (128, 1)))
    idend = np.eye(128, dtype=ml_dtypes.bfloat16)

    shared = dict(wqt=wqt, wkt=wkt, wvt=wvt, wot=wot, wbt=wbt, hct=hct,
                  bTd=bTd, bbd=bbd, btrd=btrd, idend=idend)

    ti = np.arange(128)
    in_maps = []
    for c in range(8):
        if c == 0:
            xsl = np.concatenate(
                [np.zeros((T_HALO, D), dtype=f32), x2d[0:T_OWN]], axis=0)
        else:
            xsl = np.ascontiguousarray(
                x2d[c * T_OWN - T_HALO:c * T_OWN + T_OWN])
        m = np.zeros((128, NT_OWN, WIN, 128), dtype=np.float32)
        for qt in range(NT_OWN):
            gq = c * NT_OWN + qt
            tq = gq * 128 + ti[:, None]
            for r in range(WIN):
                mg = (gq + r - 4) * 128 + ti[None, :]
                valid = (tq - mg >= 0) & (tq - mg < 512) & (mg >= 0)
                m[:, qt, r, :] = np.where(valid, 0.0, -1e30)
        im = dict(shared)
        im["xs"] = xsl
        im["mskd"] = m.astype(ml_dtypes.bfloat16)
        in_maps.append(im)
    return in_maps


_NC_CACHE = {}


def kernel(**inputs):
    in_maps = _host_layouts(**inputs)
    if "nc" not in _NC_CACHE:
        _NC_CACHE["nc"] = build_nc()
    nc = _NC_CACHE["nc"]
    res = run_bass_kernel_spmd(nc, in_maps, core_ids=list(range(8)))
    out = np.concatenate([r["y"] for r in res.results], axis=0)
    return out.reshape(1, 8192, D)


if __name__ == "__main__":
    nc = build_nc()
    print("built ok, instructions:", nc.n_instructions())
